# version 20
# baseline (speedup 1.0000x reference)
import numpy as np
import ml_dtypes
import concourse.bass as bass
import concourse.mybir as mybir
from concourse.bass_utils import run_bass_kernel_spmd

F32 = mybir.dt.float32
BF16 = mybir.dt.bfloat16
I32 = mybir.dt.int32
AF = mybir.ActivationFunctionType
ALU = mybir.AluOpType
AX = mybir.AxisListType

D = 4096
NKT = 32
NB = 128
N_CORES = 8


def build_core_program(n_tt=8, n_slab=16, slab_n=256, debug=False, sep_banks=False, skipmask=False):
    T = n_tt * 128
    assert n_slab * slab_n == D
    nbs = slab_n // 32

    import os as _os0
    _scr = int(_os0.environ.get("DMASCRATCH", "16384"))
    nc = bass.Bass("TRN2", target_bir_lowering=False, dynamic_dma_scratch_size=_scr)
    x_in = nc.dram_tensor("x_in", [T, D], F32, kind="ExternalInput").ap()
    wta = nc.dram_tensor("wta", [n_slab, 128, NKT * slab_n], F32, kind="ExternalInput").ap()
    wtb = nc.dram_tensor("wtb", [n_slab, 128, NKT * slab_n], F32, kind="ExternalInput").ap()
    mask_in = nc.dram_tensor("mask_in", [D // 8, D], I32, kind="ExternalInput").ap()
    bmg_in = nc.dram_tensor("bmg_in", [4, D // 8], BF16, kind="Internal").ap()
    bmg_out = nc.dram_tensor("bmg_out", [32, D // 8], BF16, kind="Internal").ap()
    bmscr = nc.dram_tensor("bmscr", [128, 128], BF16, kind="Internal").ap()
    at_l = nc.dram_tensor("at_l", [128, D], F32, kind="ExternalInput").ap()
    bp_l = nc.dram_tensor("bp_l", [128, D], F32, kind="ExternalInput").ap()
    wrt_l = nc.dram_tensor("wrt_l", [128, 256], F32, kind="ExternalInput").ap()
    bias_in = nc.dram_tensor("bias_in", [1, D], F32, kind="ExternalInput").ap()
    epsv_in = nc.dram_tensor("epsv_in", [128, 1], F32, kind="ExternalInput").ap()
    identb = nc.dram_tensor("identb", [128, 128], BF16, kind="ExternalInput").ap()
    ones1 = nc.dram_tensor("ones1", [1, 128], BF16, kind="ExternalInput").ap()
    bdiag4 = nc.dram_tensor("bdiag4", [128, 4], BF16, kind="ExternalInput").ap()
    s4m = nc.dram_tensor("s4m", [4, 128], BF16, kind="ExternalInput").ap()
    s16x2 = nc.dram_tensor("s16x2", [8, 128], BF16, kind="ExternalInput").ap()
    y_out = nc.dram_tensor("y_out", [T, D], F32, kind="ExternalOutput").ap()
    if debug:
        d_xres = nc.dram_tensor("d_xres", [128, n_tt * D], F32, kind="ExternalOutput").ap()
        d_stats = nc.dram_tensor("d_stats", [128, 2 * n_tt], F32, kind="ExternalOutput").ap()
        d_comb = nc.dram_tensor("d_comb", [128, 8 * n_tt], F32, kind="ExternalOutput").ap()
        d_mex = nc.dram_tensor("d_mex", [128, D], F32, kind="ExternalOutput").ap()
        d_hwt = nc.dram_tensor("d_hwt", [128, T], F32, kind="ExternalOutput").ap()
        d_xnt = nc.dram_tensor("d_xnt", [128, n_tt * NKT * 128], F32, kind="ExternalOutput").ap()
        d_csw = nc.dram_tensor("d_csw", [128, 2048], F32, kind="ExternalOutput").ap()
        d_bm4w = nc.dram_tensor("d_bm4w", [4, D], F32, kind="ExternalOutput").ap()
        d_bmt = nc.dram_tensor("d_bmt", [128, NB], F32, kind="ExternalOutput").ap()
        d_bmtw = nc.dram_tensor("d_bmtw", [4, D], F32, kind="ExternalOutput").ap()
        d_att = nc.dram_tensor("d_att", [128, 1024], F32, kind="ExternalOutput").ap()
        d_x2e = nc.dram_tensor("d_x2e", [128, n_tt * D], F32, kind="ExternalOutput").ap()

    from contextlib import ExitStack
    with ExitStack() as ctx:
        e = ctx.enter_context
        xres = e(nc.sbuf_tensor([128, n_tt, D], BF16))
        xnt = e(nc.sbuf_tensor([128, n_tt, NKT, 128], BF16))
        wts = e(nc.sbuf_tensor([128, 2, NKT, slab_n], BF16))
        r2 = e(nc.sbuf_tensor([128, 2 * D], BF16))
        xstg = e(nc.sbuf_tensor([128, D], BF16))
        ost = e(nc.sbuf_tensor([128, max(n_tt, 8), slab_n], F32))
        hwt = e(nc.sbuf_tensor([128, T], BF16))
        wrt = e(nc.sbuf_tensor([128, 256], BF16))

        logits = e(nc.sbuf_tensor([128, 8 * n_tt], F32))
        combine = e(nc.sbuf_tensor([128, 8 * n_tt], BF16))
        packed = e(nc.sbuf_tensor([128, D], BF16))
        bmt_sb = e(nc.sbuf_tensor([128, NB], BF16))
        s1s = e(nc.sbuf_tensor([128, n_tt], F32))
        s2s = e(nc.sbuf_tensor([128, n_tt], F32))
        scr = e(nc.sbuf_tensor([128, 192], F32))
        bnall = e(nc.sbuf_tensor([128, 8 * 6 * 8], F32))
        mvall = e(nc.sbuf_tensor([128, 16], F32))
        mean_t = scr[:, 0:1]; ex2_t = scr[:, 1:2]; msq_t = scr[:, 2:3]
        var_t = scr[:, 3:4]; sd_t = scr[:, 4:5]; rinv_t = scr[:, 5:6]
        negmr_t = scr[:, 6:7]; epsv = scr[:, 7:8]; m1_t = scr[:, 8:9]
        m2_t = scr[:, 9:10]; nm1_t = scr[:, 10:11]; ssum_t = scr[:, 11:12]
        rs_t = scr[:, 12:13]
        eq1_t = scr[:, 16:24]; eq2_t = scr[:, 24:32]; l2_t = scr[:, 32:40]
        eqs_t = scr[:, 40:48]; ee_t = scr[:, 48:56]; cc_t = scr[:, 56:64]
        bn_buf = [scr[:, 64:112], scr[:, 116:164]]
        mv_buf = [scr[:, 112:114], scr[:, 114:116]]
        sd_buf = [scr[:, 164:165], scr[:, 165:166]]
        ri_buf = [scr[:, 166:167], scr[:, 167:168]]
        nm_buf = [scr[:, 168:169], scr[:, 169:170]]
        id_b = e(nc.sbuf_tensor([128, 128], BF16))
        ones_sb = e(nc.sbuf_tensor([1, 128], BF16))
        bd4_sb = e(nc.sbuf_tensor([128, 4], BF16))
        s4_sb = e(nc.sbuf_tensor([4, 128], BF16))
        s16_sb = e(nc.sbuf_tensor([8, 128], BF16))
        pp = [e(nc.psum_tensor(f"pp{i}", [128, 512], F32)) for i in range(8)]
        sSm = e(nc.semaphore("sSm"))
        sXc = e(nc.semaphore("sXc"))
        sW0 = e(nc.semaphore("sW0"))
        sW1 = e(nc.semaphore("sW1"))
        sM0 = e(nc.semaphore("sM0"))
        sM1 = e(nc.semaphore("sM1"))
        sB0 = e(nc.semaphore("sB0"))
        sB1 = e(nc.semaphore("sB1"))
        sAsm = e(nc.semaphore("sAsm"))
        sAsm2 = e(nc.semaphore("sAsm2"))
        sSt = e(nc.semaphore("sSt"))
        sAt = e(nc.semaphore("sAt"))
        sWr = e(nc.semaphore("sWr"))
        sCc = e(nc.semaphore("sCc"))
        sCol = e(nc.semaphore("sCol"))
        pV = e(nc.semaphore("pV"))
        pA = e(nc.semaphore("pA"))
        pT = e(nc.semaphore("pT"))

        mview = [wts[:, b, :, :].rearrange("p a c -> p (a c)").bitcast(I32)
                 for b in range(2)]
        mexp1 = r2[:, 0:D].rearrange("p (a c) -> p a c", a=NKT)
        at3 = r2[:, 0:D].rearrange("p (a c) -> p a c", a=NKT)
        bprime = r2[:, D:2 * D]
        csw = ost[:].rearrange("p a c -> p (a c)").bitcast(BF16)
        junk = [xstg[:], xstg[:]]
        xnt7 = xnt[:, n_tt - 1].rearrange("p a b -> p (a b)") if n_tt >= 8 else None
        def ln1_src(tt):
            if n_tt < 8 or tt % 2 == 0 or tt == 7:
                return xstg[:]
            return xnt7
        xstg_ln2 = [xstg[:], ost[:].rearrange("p a c -> p (a c)").bitcast(BF16)[:, 0:D]]
        ln2_stage = wts[:, 0].rearrange("p a c -> p (a c)")
        bm4w = packed[0:4, 0:D]
        bmTw = packed[0:4, 0:D]
        bias_sb = packed[0:1, 0:D]
        combT = ost[:].rearrange("p a c -> p (a c)").bitcast(BF16)[0:8, 0:T]

        RG, RY, RV, RA, RT = [], [], [], [], []
        cnt = {"V": 0, "A": 0, "T": 0}
        dcnt = {}

        def dma(lst, sem, out_ap, in_ap, waits=(), **kw):
            def f(eng, out_ap=out_ap, in_ap=in_ap, kw=kw, waits=tuple(waits), sem=sem):
                for (s_, v_) in waits:
                    eng.wait_ge(s_, v_)
                eng.dma_start(out_ap, in_ap, **kw).then_inc(sem, 16)
            lst.append(f)
            dcnt[sem] = dcnt.get(sem, 0) + 16
            return dcnt[sem]

        def xbar(sem, out_ap, in_ap, waits=()):
            def f(eng, out_ap=out_ap, in_ap=in_ap, waits=tuple(waits), sem=sem):
                for (s_, v_) in waits:
                    eng.wait_ge(s_, v_)
                eng.dma_start(out_ap, in_ap, transpose=True).then_inc(sem, 16)
            RY.append(f)
            dcnt[sem] = dcnt.get(sem, 0) + 16
            return dcnt[sem]

        def op(lst, key, sem, builder, waits=(), selfw=True, tick=True):
            myc = cnt[key]
            do_tick = tick
            def f(eng, builder=builder, waits=tuple(waits), myc=myc, sem=sem,
                  selfw=selfw, do_tick=do_tick):
                if selfw and myc:
                    eng.wait_ge(sem, myc)
                for (s_, v_) in waits:
                    eng.wait_ge(s_, v_)
                ins = builder(eng)
                if do_tick:
                    ins.then_inc(sem, 1)
            lst.append(f)
            if do_tick:
                cnt[key] += 1
            return cnt[key]

        def vop(builder, waits=(), selfw=True, tick=True):
            return op(RV, "V", pV, builder, waits, selfw, tick)

        def aop(builder, waits=(), selfw=True, tick=True):
            return op(RA, "A", pA, builder, waits, selfw, tick)

        def top(builder, waits=(), selfw=False, tick=False):
            return op(RT, "T", pT, builder, waits, selfw, tick)

        def tmark(builder, waits=()):
            return op(RT, "T", pT, builder, waits, selfw=True, tick=True)

        c_small = 0
        for (dst, srcap) in ((epsv[:], epsv_in), (id_b[:], identb),
                           (ones_sb[:], ones1), (bd4_sb[:], bdiag4), (s4_sb[:], s4m),
                           (s16_sb[:], s16x2)):
            c_small = dma(RY, sSm, dst, srcap)
        c_wrt = dma(RG, sWr, wrt[:], wrt_l)
        xr = x_in.rearrange("(a p) k -> p a k", p=128)
        c_xcast = []
        for tt in range(n_tt):
            w = [(sXc, 16 * tt)] if tt >= 1 else ()
            c_xcast.append(dma(RG, sXc, xres[:, tt, :], xr[:, tt, :],
                               waits=w, max_dma_last_dim=8192))
        c_xcast_all = dcnt[sXc]

        def emit_ln_sqrt(tt, v_aggr, extra=()):
            pb = tt % 2
            return aop(lambda a, pb=pb: a.activation(
                sd_buf[pb][:], mv_buf[pb][:, 1:2], AF.Sqrt, bias=epsv[:]),
                waits=[(pV, v_aggr)] + list(extra))

        m2d = mask_in
        c_mload, v_red, t_bd, v_gt, t_piece = [], [], [], [], []
        ln1_xb = [None] * n_tt
        ln1_ap = [None] * n_tt
        ln1_aggr = [None] * n_tt
        ln1_sqrt = [None] * n_tt
        n_mloc = D // 8 // 128
        for i in range(n_mloc):
            qs = (i % 2 == 0)
            lst = RY if qs else RA
            sem = sM0 if qs else sM1
            w = []
            if i >= 2:
                w.append((pV, v_red[i - 2]))
            c_mload.append(dma(lst, sem, mview[i % 2],
                               m2d[128 * i:128 * (i + 1), :], waits=w))
            def _red(v, i=i):
                with nc.allow_low_precision(reason="mask block sums <=32, exact in bf16"):
                    return v.tensor_reduce(
                        csw[:, NB * i:NB * (i + 1)],
                        mview[i % 2].rearrange("p (a c) -> p a c", c=32),
                        axis=AX.X, op=ALU.add)
            v_red.append(vop(_red, waits=[(sem, c_mload[i])]))
            bank = pp[i % 2]
            w = [(pV, v_red[i])]
            if i == 0:
                w.append((sSm, c_small))
            t_bd.append(tmark(
                lambda t, i=i, bank=bank: t.matmul(
                    bank[0:4, 0:NB], bd4_sb[:], csw[:, NB * i:NB * (i + 1)],
                    start=True, stop=True, skip_group_check=True),
                waits=w))
            v_gt.append(vop(
                lambda v, i=i, bank=bank: v.tensor_single_scalar(
                    bm4w[:, NB * i:NB * (i + 1)], bank[0:4, 0:NB], 0.0, op=ALU.is_gt),
                waits=[(pT, t_bd[i])]))
        c_bmout = dma(RY, sCc, bmg_in, bm4w[:, 0:NB * n_mloc],
                      waits=[(pV, v_gt[-1])])
        def _cc(eng):
            eng.wait_ge(sCc, c_bmout)
            eng.collective_compute(
                "AllGather", ALU.bypass,
                replica_groups=[list(range(N_CORES))],
                ins=[bmg_in.opt()], outs=[bmg_out.opt()]).then_inc(sCol, 1)
        RG.append(_cc)
        c_bmin = dma(RY, sCc, bm4w[:].rearrange("j (r di kb) -> j r di kb",
                                                r=N_CORES, di=n_mloc),
                     bmg_out[:].rearrange("(r j) (di kb) -> j r di kb",
                                          j=4, kb=NB),
                     waits=[(sCol, 1)])
        c_bm_ready = c_bmin

        n_mtile = D // 128
        for i in range(n_mtile):
            t_piece.append(tmark(
                lambda t, i=i: t.matmul(
                    pp[2][:, 4 * i:4 * (i + 1)], bm4w[:, NB * i:NB * (i + 1)],
                    id_b[0:4, 0:4], start=(i == 0), stop=(i == n_mtile - 1),
                    skip_group_check=True),
                waits=([(sCc, c_bm_ready)] if i == 0 else ())))

        junkA = r2[:, D:2 * D]
        v_nm = [None] * n_tt
        a_ln1 = [None] * n_tt

        def emit_ln1_stats(tt):
            pb = tt % 2
            w = [(sXc, c_xcast[tt])]
            if tt >= 2:
                w.append((pV, v_nm[tt - 2]))
            aop(lambda a, tt=tt, pb=pb: a.activation(
                junkA, xres[:, tt, :], AF.Copy, accum_out=mv_buf[pb][:, 0:1]),
                waits=w)
            a_ln1[tt] = aop(lambda a, tt=tt, pb=pb: a.activation(
                junkA, xres[:, tt, :], AF.Square, accum_out=mv_buf[pb][:, 1:2]))
            vop(lambda v, pb=pb: v.tensor_single_scalar(
                mv_buf[pb][:, 0:1], mv_buf[pb][:, 0:1], 1.0 / D, op=ALU.mult),
                waits=[(pA, a_ln1[tt])])
            vop(lambda v, pb=pb: v.tensor_tensor(
                sd_buf[pb][:], mv_buf[pb][:, 0:1], mv_buf[pb][:, 0:1],
                op=ALU.mult))
            ln1_aggr[tt] = vop(lambda v, pb=pb: v.scalar_tensor_tensor(
                mv_buf[pb][:, 1:2], mv_buf[pb][:, 1:2], 1.0 / D, sd_buf[pb][:],
                op0=ALU.mult, op1=ALU.subtract))

        emit_ln1_stats(0)
        if n_tt > 1:
            emit_ln1_stats(1)
        for tt in range(n_tt):
            ln1_sqrt[tt] = emit_ln_sqrt(tt, ln1_aggr[tt],
                                        extra=([(sSm, c_small)] if tt == 0 else ()))
            pb = tt % 2
            vop(lambda v, pb=pb: v.reciprocal(ri_buf[pb][:], sd_buf[pb][:]),
                waits=[(pA, ln1_sqrt[tt])])
            v_nm[tt] = vop(lambda v, pb=pb: v.tensor_scalar(
                nm_buf[pb][:], mv_buf[pb][:, 0:1], ri_buf[pb][:], -1.0,
                op0=ALU.mult, op1=ALU.mult))
            wap = []
            if n_tt < 8:
                if tt >= 1:
                    wap.append(((sB0 if (tt - 1) % 2 == 0 else sB1), ln1_xb[tt - 1]))
            elif tt >= 2:
                wap.append(((sB0 if tt % 2 == 0 else sB1), ln1_xb[tt - 2]))
            if tt == 7 and n_tt >= 8:
                wap.append((sB0, ln1_xb[6]))
            ln1_ap[tt] = vop(lambda v, tt=tt, pb=pb: v.tensor_scalar(
                ln1_src(tt), xres[:, tt, :], ri_buf[pb][:], nm_buf[pb][:],
                op0=ALU.mult, op1=ALU.add), waits=wap)
            w = [(pV, ln1_ap[tt])]
            ln1_xb[tt] = xbar((sB0 if tt % 2 == 0 else sB1),
                              xnt[:, tt, :, :], ln1_src(tt), waits=w)
            if tt + 2 < n_tt:
                emit_ln1_stats(tt + 2)
        ln1_b0, ln1_b1 = dcnt.get(sB0, 0), dcnt.get(sB1, 0)

        v_bmt = vop(lambda v: v.tensor_copy(bmt_sb[:], pp[2][:, 0:NB]),
                    waits=[(pT, t_piece[-1])])
        c_asm1 = dma(RY, sAsm, bmscr, bmt_sb[:], waits=[(pV, v_bmt)])
        c_asm = dma(RY, sAsm, bmTw[:].rearrange("k (kt j) -> k kt j", j=NB),
                    bmscr.rearrange("(kt k) j -> k kt j", k=4),
                    waits=[(sAsm, c_asm1)])
        t_mex, v_mex = [], []
        for c in range(NKT // 4):
            bank = pp[3 + (c % 2)]
            w = [(sAsm, c_asm)] if c == 0 else ()
            if c >= 2:
                w = list(w) + [(pV, v_mex[c - 2])]
            t_mex.append(tmark(
                lambda t, c=c, bank=bank: t.matmul(
                    bank[:], s4_sb[:], bmTw[:, 512 * c:512 * (c + 1)],
                    start=True, stop=True, skip_group_check=True),
                waits=w))
            v_mex.append(vop(
                lambda v, c=c, bank=bank: v.tensor_copy(
                    mexp1[:, 4 * c:4 * (c + 1), :], bank[:]),
                waits=[(pT, t_mex[c])]))
        v_mask_done = v_mex[-1]

        if debug:
            wdm = [(pV, cnt["V"]), (pT, cnt["T"]),
                   (sB0, ln1_b0), (sB1, ln1_b1)]
            c_dm = dma(RG, sAt, d_bm4w, bm4w, waits=wdm)
            c_dm = dma(RG, sAt, d_bmt, bmt_sb[:], waits=wdm)
            c_dm = dma(RG, sAt, d_bmtw, bmTw, waits=wdm)
            c_dm = dma(RG, sAt, d_csw, csw[:, 0:2048], waits=wdm)
            c_dm = dma(RG, sAt, d_mex, mexp1[:].rearrange("p a c -> p (a c)"),
                       waits=[(pV, cnt["V"])])
            c_dm = dma(RG, sAt, d_xnt, xnt[:].rearrange("p a b c -> p (a b c)"),
                       waits=[(sB0, ln1_b0), (sB1, ln1_b1)], max_dma_last_dim=4096)
            dbg_pre_attn = c_dm

        t_slab = []
        v_evac = []
        c_wcast = []

        def emit_attn_apply(s):
            b = s % 2
            w = [((sW0 if b == 0 else sW1), c_wcast[s])]
            if s < 2:
                w.append((pV, v_mask_done))
            else:
                w.append((pT, t_slab[s - 2]))
            return vop(lambda v, b=b, s=s: v.tensor_tensor(
                wts[:, b].rearrange("p a (j r) -> p a j r", r=32),
                wts[:, b].rearrange("p a (j r) -> p a j r", r=32),
                mexp1[:, :, nbs * s:nbs * (s + 1)].unsqueeze(3)
                    .broadcast_to([128, NKT, nbs, 32]),
                op=ALU.mult), waits=w)

        def emit_attn_evac(s):
            b = s % 2
            last = None
            for tt in range(n_tt):
                bank = pp[4 * b + tt // 2]
                c0 = 256 * (tt % 2)
                last = vop(lambda v, tt=tt, s=s, bank=bank, c0=c0: v.tensor_tensor(
                    xres[:, tt, slab_n * s:slab_n * (s + 1)],
                    bank[:, c0:c0 + slab_n],
                    xres[:, tt, slab_n * s:slab_n * (s + 1)], op=ALU.add),
                    waits=([(pT, t_slab[s])] if tt == 0 else ()),
                    selfw=(tt == 0), tick=(tt == n_tt - 1))
            v_evac.append(last)

        v_applies = []
        for s in range(n_slab):
            b = s % 2
            if s < 2:
                w = [(pV, v_red[n_mloc - 2 + s]), (sXc, c_xcast[n_tt - 1])]
                if debug:
                    w.append((sAt, dbg_pre_attn))
            else:
                w = [(pT, t_slab[s - 2])]
            c_wcast.append(dma(RG, (sW0 if b == 0 else sW1),
                               wts[:, b, :, :].rearrange("p a c -> p (a c)"),
                               wta[s], waits=w, max_dma_last_dim=8192))
            v_applies.append(emit_attn_apply(s))
            if s >= 1:
                pass
            w = [(pV, v_applies[s])]
            if s == 0:
                w.append((sB0, ln1_b0))
                w.append((sB1, ln1_b1))
            if s >= 2:
                w.append((pV, v_evac[s - 2]))
            first = True
            for kt in range(NKT):
                for tt in range(n_tt):
                    bank = pp[4 * b + tt // 2]
                    c0 = 256 * (tt % 2)
                    st = (kt == 0 and tt % 2 == 0)
                    mark = (kt == NKT - 1 and tt == n_tt - 1)
                    f = (lambda t, kt=kt, tt=tt, b=b, bank=bank, c0=c0, st=st: t.matmul(
                        bank[:, c0:c0 + slab_n],
                        xnt[:, tt, kt, :], wts[:, b, kt, :],
                        start=st, stop=(kt == NKT - 1),
                        skip_group_check=True))
                    if mark:
                        t_slab.append(tmark(f, waits=(w if first else ())))
                    else:
                        top(f, waits=(w if first else ()))
                    first = False
            if debug and s == 0:
                for tt in range(n_tt):
                    bank = pp[4 * b + tt // 2]
                    c0 = 256 * (tt % 2)
                    vop(lambda v, tt=tt, bank=bank, c0=c0: v.tensor_copy(
                        ost[:, 4 + tt, :], bank[:, c0:c0 + slab_n]),
                        waits=[(pT, t_slab[0])])
                v_datt = cnt["V"]
                c_datt = dma(RG, sAt, d_att[:, 0:n_tt * slab_n],
                             ost[:, 4:4 + n_tt, :], waits=[(pV, v_datt)])
            if s >= 1:
                emit_attn_evac(s - 1)
                if (s - 1) % 2 == 1 and slab_n == 256:
                    jj = (s - 1) // 2
                    for tt in range(n_tt):
                        vop(lambda v, tt=tt, jj=jj: v.bn_stats(
                            bnall[:, 48 * tt + 6 * jj:48 * tt + 6 * (jj + 1)],
                            xres[:, tt, 512 * jj:512 * (jj + 1)]),
                            selfw=(tt == 0), tick=(tt == n_tt - 1))
        emit_attn_evac(n_slab - 1)
        if slab_n == 256:
            jj = (n_slab - 1) // 2
            for tt in range(n_tt):
                vop(lambda v, tt=tt, jj=jj: v.bn_stats(
                    bnall[:, 48 * tt + 6 * jj:48 * tt + 6 * (jj + 1)],
                    xres[:, tt, 512 * jj:512 * (jj + 1)]),
                    selfw=(tt == 0), tick=(tt == n_tt - 1))
        v_attn_done = cnt["V"]
        t_attn_done = t_slab[-1]

        if debug:
            import os as _os
            _gate = int(_os.environ.get("X2GATE", str(n_slab - 1)))
            c_x2e = dma(RG, sAt, d_x2e, xres[:].rearrange("p a c -> p (a c)"),
                        waits=[(pV, v_evac[_gate]), (pT, t_attn_done) if _gate == n_slab - 1 else (pT, t_slab[_gate])],
                        max_dma_last_dim=4096)
            d_xp = nc.dram_tensor("d_xp", [128, n_tt * D], F32, kind="ExternalOutput").ap()
            _w = _os.environ.get("XPW", "ln1")
            if _w == "ln1":
                _wl = [(pV, ln1_ap[-1])]
            elif _w == "xbar":
                _wl = [(sB0, ln1_b0), (sB1, ln1_b1)]
            elif _w == "mask":
                _wl = [(pV, v_mask_done)]
            else:
                _wl = [(pV, v_apply)]
            if _w != "apply":
                dma(RG, sAt, d_xp, xres[:].rearrange("p a c -> p (a c)"),
                    waits=_wl, max_dma_last_dim=4096)
        ln2_ap = [None] * n_tt
        ln2_xb = [None] * n_tt
        ln2_aggr = [None] * n_tt

        c_at = dma(RG, sAt, at3.rearrange("p a c -> p (a c)"), at_l[:],
                   waits=[(pV, v_applies[-1])], max_dma_last_dim=8192)
        c_bp = dma(RG, sAt, bprime, bp_l[:], max_dma_last_dim=8192)
        c_bias = dma(RG, sAt, bias_sb[:, :], bias_in)

        tw = min(4, n_tt)
        nth = n_tt // tw
        nh = tw * 128
        t_rt = None
        t_ht = None

        def emit_router(tt):
            nonlocal t_rt
            sbp = sB0 if tt % 2 == 0 else sB1
            for kc in range(NKT):
                mark = (tt == n_tt - 1 and kc == NKT - 1)
                w = [(sbp, ln2_xb[tt])] if kc == 0 else ()
                if tt == 0 and kc == 0:
                    w = [(sbp, ln2_xb[tt]), (sWr, c_wrt), (pV, v_attn_done)]
                f = (lambda t, tt=tt, kc=kc: t.matmul(
                    pp[0][:, 8 * tt:8 * (tt + 1)], xnt[:, tt, kc, :],
                    wrt[:, 8 * kc:8 * (kc + 1)],
                    start=(tt == 0 and kc == 0), stop=(kc == NKT - 1),
                    skip_group_check=True))
                if mark:
                    t_rt = tmark(f, waits=w)
                else:
                    top(f, waits=w)

        def emit_ht(th):
            nonlocal t_ht
            for kc in range(NKT):
                mark = (th == nth - 1 and kc == NKT - 1)
                w = [(sAt, c_bias)] if (th == 0 and kc == 0) else ()
                f = (lambda t, th=th, kc=kc: t.matmul(
                    pp[1 + th][:, 0:nh], at3[:, kc, :],
                    xnt[:, tw * th:tw * (th + 1), kc, :], start=(kc == 0),
                    stop=(kc == NKT - 1), skip_group_check=True))
                if mark:
                    t_ht = tmark(f, waits=w)
                else:
                    top(f, waits=w)

        for tt in range(n_tt):
            wag = []
            if tt == 0:
                wag = [(pV, v_attn_done)]
                if debug:
                    wag.append((sAt, c_x2e))
            if slab_n == 256:
                ln2_aggr[tt] = vop(lambda v, tt=tt: v.bn_aggr(
                    mvall[:, 2 * tt:2 * (tt + 1)], bnall[:, 48 * tt:48 * (tt + 1)]),
                    waits=wag)
            else:
                pb = tt % 2
                for j in range(D // 512):
                    vop(lambda v, tt=tt, j=j, pb=pb: v.bn_stats(
                        bn_buf[pb][:, 6 * j:6 * (j + 1)],
                        xres[:, tt, 512 * j:512 * (j + 1)]),
                        waits=(wag if j == 0 else ()), selfw=(j == 0),
                        tick=(j == D // 512 - 1))
                ln2_aggr[tt] = vop(lambda v, tt=tt, pb=pb: v.bn_aggr(
                    mvall[:, 2 * tt:2 * (tt + 1)], bn_buf[pb][:]))
            pb = tt % 2
            a_sd = aop(lambda a, tt=tt, pb=pb: a.activation(
                sd_buf[pb][:], mvall[:, 2 * tt + 1:2 * tt + 2], AF.Sqrt, bias=epsv[:]),
                waits=[(pV, ln2_aggr[tt])])
            vop(lambda v, pb=pb: v.reciprocal(ri_buf[pb][:], sd_buf[pb][:]),
                waits=[(pA, a_sd)])
            vop(lambda v, tt=tt, pb=pb: v.tensor_scalar(
                nm_buf[pb][:], mvall[:, 2 * tt:2 * tt + 1], ri_buf[pb][:], -1.0,
                op0=ALU.mult, op1=ALU.mult))
            wap = []
            if tt >= 2:
                wap.append(((sB0 if tt % 2 == 0 else sB1), ln2_xb[tt - 2]))
            ln2_ap[tt] = vop(lambda v, tt=tt, pb=pb: v.tensor_scalar(
                xstg_ln2[pb], xres[:, tt, :], ri_buf[pb][:], nm_buf[pb][:],
                op0=ALU.mult, op1=ALU.add), waits=wap)
            w = [(pV, ln2_ap[tt]), (pT, t_attn_done)]
            ln2_xb[tt] = xbar((sB0 if tt % 2 == 0 else sB1),
                              xnt[:, tt, :, :], xstg_ln2[tt % 2], waits=w)
            emit_router(tt)
            if (tt + 1) % tw == 0:
                emit_ht((tt + 1) // tw - 1)
        ln2_b0, ln2_b1 = dcnt.get(sB0, 0), dcnt.get(sB1, 0)

        v_lg = vop(lambda v: v.tensor_copy(logits[:, 0:8 * n_tt], pp[0][:, 0:8 * n_tt]),
                   waits=[(pT, t_rt)])
        v_cmb = []
        for tt in range(n_tt):
            lg = logits[:, 8 * tt:8 * (tt + 1)]
            vop(lambda v, lg=lg: v.reduce_max(m1_t[:], lg, axis=AX.X))
            vop(lambda v, lg=lg: v.tensor_scalar(eq1_t[:], lg, m1_t[:], None,
                                                 op0=ALU.is_equal))
            vop(lambda v, lg=lg: v.scalar_tensor_tensor(l2_t[:], eq1_t[:], -1e30, lg,
                                                        op0=ALU.mult, op1=ALU.add))
            vop(lambda v: v.reduce_max(m2_t[:], l2_t[:], axis=AX.X))
            vop(lambda v: v.tensor_scalar(eq2_t[:], l2_t[:], m2_t[:], None,
                                          op0=ALU.is_equal))
            vop(lambda v: v.tensor_tensor(eqs_t[:], eq1_t[:], eq2_t[:], op=ALU.add))
            v_nm = vop(lambda v: v.tensor_scalar(nm1_t[:], m1_t[:], -1.0, None,
                                                 op0=ALU.mult))
            a_e = aop(lambda a, lg=lg: a.activation(ee_t[:], lg, AF.Exp, bias=nm1_t[:]),
                      waits=[(pV, v_nm)])
            vop(lambda v: v.tensor_tensor(cc_t[:], ee_t[:], eqs_t[:], op=ALU.mult),
                waits=[(pA, a_e)])
            vop(lambda v: v.reduce_sum(ssum_t[:], cc_t[:], axis=AX.X))
            vop(lambda v: v.reciprocal(rs_t[:], ssum_t[:]))
            v_cmb.append(vop(lambda v, tt=tt: v.tensor_scalar(
                combine[:, 8 * tt:8 * (tt + 1)], cc_t[:], rs_t[:], None, op0=ALU.mult)))
        ct_bank = [pp[3], pp[0]]
        t_ct = []
        for tt in range(n_tt):
            bank = ct_bank[tt // tw]
            w = [(pV, v_cmb[tt])]
            if tt == tw:
                w.append((pV, v_lg))
            t_ct.append(tmark(lambda t, tt=tt, bank=bank: t.matmul(
                bank[0:8, 128 * (tt % tw):128 * (tt % tw + 1)],
                combine[:, 8 * tt:8 * (tt + 1)], id_b[:],
                start=(tt % tw == 0), stop=(tt % tw == tw - 1 or tt == n_tt - 1),
                skip_group_check=True), waits=w))
        v_ct = []
        for h in range(nth):
            v_ct.append(vop(lambda v, h=h: v.tensor_copy(
                combT[0:8, nh * h:nh * (h + 1)], ct_bank[h][0:8, 0:nh]),
                waits=[(pT, t_ct[min(tw * h + tw - 1, n_tt - 1)])]))
        t_cte, v_cte = [], []
        for th in range(nth):
            w = [(pV, v_ct[th])]
            if th >= 1:
                w.append((pV, v_cte[th - 1]))
            t_cte.append(tmark(lambda t, th=th: t.matmul(
                pp[3][:, 0:nh], s16_sb[:], combT[0:8, nh * th:nh * (th + 1)],
                start=True, stop=True, skip_group_check=True),
                waits=w))
            v_cte.append(vop(lambda v, th=th: v.tensor_copy(
                hwt[:, nh * th:nh * (th + 1)], pp[3][:, 0:nh]),
                waits=[(pT, t_cte[th])]))
        v_hwt = []
        for th in range(nth):
            v_hwt.append(vop(lambda v, th=th: v.tensor_tensor(
                hwt[:, nh * th:nh * (th + 1)], pp[1 + th][:, 0:nh],
                hwt[:, nh * th:nh * (th + 1)], op=ALU.mult),
                waits=[(pT, t_ht), (pV, v_cte[th])]))
        v_prep_done = v_hwt[-1]

        t_fslab, v_fevac, c_store = [], [], []
        y2 = y_out.rearrange("(a p) k -> p a k", p=128)
        for s in range(n_slab):
            b = s % 2
            w = [(sB0, ln2_b0), (sB1, ln2_b1), (sAt, c_bias)]
            if s >= 2:
                w.append((pT, t_fslab[s - 2]))
            else:
                w.append((pT, t_attn_done))
            c_w = dma(RG, (sW0 if b == 0 else sW1),
                      wts[:, b, :, :].rearrange("p a c -> p (a c)"),
                      wtb[s], waits=w, max_dma_last_dim=8192)
            c_wcast.append(c_w)
            bps = (s + 1) % 2
            w = [((sW0 if b == 0 else sW1), c_w)]
            if s >= 2:
                w.append((pV, v_fevac[s - 2]))
            elif s == 0:
                w.append((pV, v_attn_done))
            else:
                w.append((pV, v_prep_done))
            first = True
            for kt in range(NKT):
                for tt in range(n_tt):
                    bank = pp[4 * bps + tt // 2]
                    top(lambda t, kt=kt, tt=tt, b=b, bank=bank: t.matmul(
                        bank[:, 256 * (tt % 2):256 * (tt % 2) + slab_n],
                        xnt[:, tt, kt, :], wts[:, b, kt, :],
                        start=(kt == 0 and tt % 2 == 0), stop=False,
                        skip_group_check=True), waits=(w if first else ()))
                    first = False
            for tt in range(n_tt):
                bank = pp[4 * bps + tt // 2]
                wl = [(pV, v_prep_done)] if (s == 0 and tt == 0) else ()
                top(lambda t, tt=tt, s=s, bank=bank: t.matmul(
                    bank[:, 256 * (tt % 2):256 * (tt % 2) + slab_n],
                    hwt[:, 128 * tt:128 * (tt + 1)],
                    bprime[:, slab_n * s:slab_n * (s + 1)],
                    start=False, stop=False, skip_group_check=True), waits=wl)
                mark = (tt == n_tt - 1)
                f = (lambda t, tt=tt, s=s, bank=bank: t.matmul(
                    bank[:, 256 * (tt % 2):256 * (tt % 2) + slab_n],
                    ones_sb[:], bias_sb[:, slab_n * s:slab_n * (s + 1)],
                    start=False, stop=True, skip_group_check=True))
                if mark:
                    t_fslab.append(tmark(f))
                else:
                    top(f)
            last = None
            for tt in range(n_tt):
                bank = pp[4 * bps + tt // 2]
                w2 = []
                if tt == 0:
                    w2 = [(pT, t_fslab[s])]
                    if s >= 1:
                        w2.append((sSt, c_store[s - 1]))
                last = vop(lambda v, tt=tt, s=s, bank=bank: v.tensor_tensor(
                    ost[:, tt, :], bank[:, 256 * (tt % 2):256 * (tt % 2) + slab_n],
                    xres[:, tt, slab_n * s:slab_n * (s + 1)], op=ALU.add),
                    waits=w2, selfw=(tt == 0), tick=(tt == n_tt - 1))
            v_fevac.append(last)
            c_store.append(dma(RY, sSt,
                               y2[:, :, slab_n * s:slab_n * (s + 1)], ost[:, 0:n_tt, :],
                               waits=[(pV, v_fevac[s])]))
        c_final = dcnt[sSt]
        if debug:
            wfin = [(sSt, c_final), (pV, cnt["V"]), (pA, cnt["A"]), (pT, cnt["T"])]
            dma(RG, sAt, d_xres, xres[:].rearrange("p a c -> p (a c)"),
                waits=wfin, max_dma_last_dim=4096)
            dma(RG, sAt, d_comb, combine[:], waits=wfin)
            dma(RG, sAt, d_hwt, hwt[:], waits=wfin)
            c_final2 = dcnt[sAt]

        with nc.Block() as blk:
            @blk.gpsimd
            def _(g):
                for f in RG:
                    f(g)

            @blk.sync
            def _(s):
                for f in RY:
                    f(s)
                s.wait_ge(sSt, c_final)
                if debug:
                    s.wait_ge(sAt, c_final2)

            @blk.vector
            def _(v):
                for f in RV:
                    f(v)

            @blk.scalar
            def _(a):
                for f in RA:
                    f(a)

            @blk.tensor
            def _(t):
                for f in RT:
                    f(t)

    return nc


def _lay_weights(attn_w, base_w, a_w, b_w, wr_w, n_slab=16, slab_n=256):
    wt = np.ascontiguousarray(
        attn_w.T.reshape(NKT, 128, n_slab, slab_n).transpose(2, 1, 0, 3)
    ).reshape(n_slab, 128, NKT * slab_n)
    wtb = np.ascontiguousarray(
        base_w.T.reshape(NKT, 128, n_slab, slab_n).transpose(2, 1, 0, 3)
    ).reshape(n_slab, 128, NKT * slab_n)
    at = np.ascontiguousarray(
        a_w.reshape(128, NKT, 128).transpose(2, 1, 0)).reshape(128, D)
    bp = np.ascontiguousarray(b_w.transpose(0, 2, 1)).reshape(128, D)
    wrt = np.ascontiguousarray(
        wr_w.reshape(8, NKT, 128).transpose(2, 1, 0)).reshape(128, NKT * 8)
    return wt, wtb, at, bp, wrt


def _consts():
    p = np.arange(128)
    identb = np.eye(128, dtype=np.float32).astype(ml_dtypes.bfloat16)
    identf = None
    ones1 = np.ones((1, 128), dtype=ml_dtypes.bfloat16)
    bdiag4 = (p[:, None] // 32 == np.arange(4)[None, :]).astype(np.float32).astype(ml_dtypes.bfloat16)
    s4m = (np.arange(4)[:, None] == p[None, :] // 32).astype(np.float32).astype(ml_dtypes.bfloat16)
    s16 = (2.0 * (np.arange(8)[:, None] == p[None, :] // 16)).astype(np.float32).astype(ml_dtypes.bfloat16)
    epsv = np.full((128, 1), 1e-5, dtype=np.float32)
    return identb, identf, ones1, bdiag4, s4m, s16, epsv


def kernel(L0_attn_data, L0_attn_weight, L0_attn_mask, L0_ffn_W_base,
           L0_ffn_b_base, L0_ffn_W_router, L0_ffn_A, L0_ffn_B):
    x2d = np.ascontiguousarray(np.asarray(L0_attn_data, dtype=np.float32).reshape(8192, D))
    wt, wtb, at, bp, wrt = _lay_weights(
        np.asarray(L0_attn_weight, dtype=np.float32),
        np.asarray(L0_ffn_W_base, dtype=np.float32),
        np.asarray(L0_ffn_A, dtype=np.float32).reshape(128, D),
        np.asarray(L0_ffn_B, dtype=np.float32),
        np.asarray(L0_ffn_W_router, dtype=np.float32))
    mask = np.asarray(L0_attn_mask, dtype=np.int32)
    bias = np.asarray(L0_ffn_b_base, dtype=np.float32).reshape(1, D)
    identb, identf, ones1, bdiag4, s4m, s16, epsv = _consts()

    nc = build_core_program()
    shared = dict(wta=wt, wtb=wtb, at_l=at, bp_l=bp, wrt_l=wrt,
                  bias_in=bias, epsv_in=epsv, identb=identb,
                  ones1=ones1, bdiag4=bdiag4, s4m=s4m, s16x2=s16)
    in_maps = []
    for c in range(N_CORES):
        m = dict(shared)
        m["x_in"] = np.ascontiguousarray(x2d[1024 * c:1024 * (c + 1), :])
        m["mask_in"] = np.ascontiguousarray(mask[512 * c:512 * (c + 1), :])
        in_maps.append(m)
    res = run_bass_kernel_spmd(nc, in_maps, core_ids=list(range(N_CORES)))
    out = np.concatenate([r["y_out"] for r in res.results], axis=0)
    return out.reshape(2, 4096, 4096)


# version 21
# speedup vs baseline: 1.0215x; 1.0215x over previous
import numpy as np
import ml_dtypes
import concourse.bass as bass
import concourse.mybir as mybir
from concourse.bass_utils import run_bass_kernel_spmd

F32 = mybir.dt.float32
BF16 = mybir.dt.bfloat16
I32 = mybir.dt.int32
AF = mybir.ActivationFunctionType
ALU = mybir.AluOpType
AX = mybir.AxisListType

D = 4096
NKT = 32
NB = 128
N_CORES = 8


def build_core_program(n_tt=8, n_slab=16, slab_n=256, debug=False, sep_banks=False, skipmask=False):
    T = n_tt * 128
    assert n_slab * slab_n == D
    nbs = slab_n // 32

    import os as _os0
    _scr = int(_os0.environ.get("DMASCRATCH", "16384"))
    nc = bass.Bass("TRN2", target_bir_lowering=False, dynamic_dma_scratch_size=_scr)
    x_in = nc.dram_tensor("x_in", [T, D], F32, kind="ExternalInput").ap()
    wta = nc.dram_tensor("wta", [n_slab, 128, NKT * slab_n], F32, kind="ExternalInput").ap()
    wtb = nc.dram_tensor("wtb", [n_slab, 128, NKT * slab_n], F32, kind="ExternalInput").ap()
    mask_in = nc.dram_tensor("mask_in", [D // 8, D], I32, kind="ExternalInput").ap()
    bmg_in = nc.dram_tensor("bmg_in", [4, D // 8], BF16, kind="Internal").ap()
    bmg_out = nc.dram_tensor("bmg_out", [32, D // 8], BF16, kind="Internal").ap()
    bmscr = nc.dram_tensor("bmscr", [128, 128], BF16, kind="Internal").ap()
    at_l = nc.dram_tensor("at_l", [128, D], F32, kind="ExternalInput").ap()
    bp_l = nc.dram_tensor("bp_l", [128, D], F32, kind="ExternalInput").ap()
    wrt_l = nc.dram_tensor("wrt_l", [128, 256], F32, kind="ExternalInput").ap()
    bias_in = nc.dram_tensor("bias_in", [1, D], F32, kind="ExternalInput").ap()
    epsv_in = nc.dram_tensor("epsv_in", [128, 1], F32, kind="ExternalInput").ap()
    identb = nc.dram_tensor("identb", [128, 128], BF16, kind="ExternalInput").ap()
    ones1 = nc.dram_tensor("ones1", [1, 128], BF16, kind="ExternalInput").ap()
    bdiag4 = nc.dram_tensor("bdiag4", [128, 4], BF16, kind="ExternalInput").ap()
    s4m = nc.dram_tensor("s4m", [4, 128], BF16, kind="ExternalInput").ap()
    s16x2 = nc.dram_tensor("s16x2", [8, 128], BF16, kind="ExternalInput").ap()
    y_out = nc.dram_tensor("y_out", [T, D], F32, kind="ExternalOutput").ap()
    if debug:
        d_xres = nc.dram_tensor("d_xres", [128, n_tt * D], F32, kind="ExternalOutput").ap()
        d_stats = nc.dram_tensor("d_stats", [128, 2 * n_tt], F32, kind="ExternalOutput").ap()
        d_comb = nc.dram_tensor("d_comb", [128, 8 * n_tt], F32, kind="ExternalOutput").ap()
        d_mex = nc.dram_tensor("d_mex", [128, D], F32, kind="ExternalOutput").ap()
        d_hwt = nc.dram_tensor("d_hwt", [128, T], F32, kind="ExternalOutput").ap()
        d_xnt = nc.dram_tensor("d_xnt", [128, n_tt * NKT * 128], F32, kind="ExternalOutput").ap()
        d_csw = nc.dram_tensor("d_csw", [128, 2048], F32, kind="ExternalOutput").ap()
        d_bm4w = nc.dram_tensor("d_bm4w", [4, D], F32, kind="ExternalOutput").ap()
        d_bmt = nc.dram_tensor("d_bmt", [128, NB], F32, kind="ExternalOutput").ap()
        d_bmtw = nc.dram_tensor("d_bmtw", [4, D], F32, kind="ExternalOutput").ap()
        d_att = nc.dram_tensor("d_att", [128, 1024], F32, kind="ExternalOutput").ap()
        d_x2e = nc.dram_tensor("d_x2e", [128, n_tt * D], F32, kind="ExternalOutput").ap()

    from contextlib import ExitStack
    with ExitStack() as ctx:
        e = ctx.enter_context
        xres = e(nc.sbuf_tensor([128, n_tt, D], BF16))
        xnt = e(nc.sbuf_tensor([128, n_tt, NKT, 128], BF16))
        wts = e(nc.sbuf_tensor([128, 2, NKT, slab_n], BF16))
        r2 = e(nc.sbuf_tensor([128, 2 * D], BF16))
        xstg = e(nc.sbuf_tensor([128, D], BF16))
        ost = e(nc.sbuf_tensor([128, max(n_tt, 8), slab_n], F32))
        hwt = e(nc.sbuf_tensor([128, T], BF16))
        wrt = e(nc.sbuf_tensor([128, 256], BF16))

        logits = e(nc.sbuf_tensor([128, 8 * n_tt], F32))
        combine = e(nc.sbuf_tensor([128, 8 * n_tt], BF16))
        packed = e(nc.sbuf_tensor([128, D], BF16))
        bmt_sb = e(nc.sbuf_tensor([128, NB], BF16))
        s1s = e(nc.sbuf_tensor([128, n_tt], F32))
        s2s = e(nc.sbuf_tensor([128, n_tt], F32))
        scr = e(nc.sbuf_tensor([128, 192], F32))
        bnall = e(nc.sbuf_tensor([128, 8 * 6 * 8], F32))
        mvall = e(nc.sbuf_tensor([128, 16], F32))
        mean_t = scr[:, 0:1]; ex2_t = scr[:, 1:2]; msq_t = scr[:, 2:3]
        var_t = scr[:, 3:4]; sd_t = scr[:, 4:5]; rinv_t = scr[:, 5:6]
        negmr_t = scr[:, 6:7]; epsv = scr[:, 7:8]; m1_t = scr[:, 8:9]
        m2_t = scr[:, 9:10]; nm1_t = scr[:, 10:11]; ssum_t = scr[:, 11:12]
        rs_t = scr[:, 12:13]
        eq1_t = scr[:, 16:24]; eq2_t = scr[:, 24:32]; l2_t = scr[:, 32:40]
        eqs_t = scr[:, 40:48]; ee_t = scr[:, 48:56]; cc_t = scr[:, 56:64]
        bn_buf = [scr[:, 64:112], scr[:, 116:164]]
        mv_buf = [scr[:, 112:114], scr[:, 114:116]]
        sd_buf = [scr[:, 164:165], scr[:, 165:166]]
        ri_buf = [scr[:, 166:167], scr[:, 167:168]]
        nm_buf = [scr[:, 168:169], scr[:, 169:170]]
        id_b = e(nc.sbuf_tensor([128, 128], BF16))
        ones_sb = e(nc.sbuf_tensor([1, 128], BF16))
        bd4_sb = e(nc.sbuf_tensor([128, 4], BF16))
        s4_sb = e(nc.sbuf_tensor([4, 128], BF16))
        s16_sb = e(nc.sbuf_tensor([8, 128], BF16))
        pp = [e(nc.psum_tensor(f"pp{i}", [128, 512], F32)) for i in range(8)]
        sSm = e(nc.semaphore("sSm"))
        sXc = e(nc.semaphore("sXc"))
        sW0 = e(nc.semaphore("sW0"))
        sW1 = e(nc.semaphore("sW1"))
        sM0 = e(nc.semaphore("sM0"))
        sM1 = e(nc.semaphore("sM1"))
        sB0 = e(nc.semaphore("sB0"))
        sB1 = e(nc.semaphore("sB1"))
        sAsm = e(nc.semaphore("sAsm"))
        sAsm2 = e(nc.semaphore("sAsm2"))
        sSt = e(nc.semaphore("sSt"))
        sAt = e(nc.semaphore("sAt"))
        sWr = e(nc.semaphore("sWr"))
        sCc = e(nc.semaphore("sCc"))
        sCol = e(nc.semaphore("sCol"))
        pV = e(nc.semaphore("pV"))
        pA = e(nc.semaphore("pA"))
        pT = e(nc.semaphore("pT"))

        mview = [wts[:, b, :, :].rearrange("p a c -> p (a c)").bitcast(I32)
                 for b in range(2)]
        mexp1 = r2[:, 0:D].rearrange("p (a c) -> p a c", a=NKT)
        at3 = r2[:, 0:D].rearrange("p (a c) -> p a c", a=NKT)
        bprime = r2[:, D:2 * D]
        csw = ost[:].rearrange("p a c -> p (a c)").bitcast(BF16)
        junk = [xstg[:], xstg[:]]
        xnt7 = xnt[:, n_tt - 1].rearrange("p a b -> p (a b)") if n_tt >= 8 else None
        def ln1_src(tt):
            if n_tt < 8 or tt % 2 == 0 or tt == 7:
                return xstg[:]
            return xnt7
        xstg_ln2 = [xstg[:], ost[:].rearrange("p a c -> p (a c)").bitcast(BF16)[:, 0:D]]
        ln2_stage = wts[:, 0].rearrange("p a c -> p (a c)")
        bm4w = packed[0:4, 0:D]
        bmTw = packed[0:4, 0:D]
        bias_sb = packed[0:1, 0:D]
        combT = ost[:].rearrange("p a c -> p (a c)").bitcast(BF16)[0:8, 0:T]

        RG, RY, RV, RA, RT = [], [], [], [], []
        cnt = {"V": 0, "A": 0, "T": 0}
        dcnt = {}

        def dma(lst, sem, out_ap, in_ap, waits=(), **kw):
            def f(eng, out_ap=out_ap, in_ap=in_ap, kw=kw, waits=tuple(waits), sem=sem):
                for (s_, v_) in waits:
                    eng.wait_ge(s_, v_)
                eng.dma_start(out_ap, in_ap, **kw).then_inc(sem, 16)
            lst.append(f)
            dcnt[sem] = dcnt.get(sem, 0) + 16
            return dcnt[sem]

        def xbar(sem, out_ap, in_ap, waits=()):
            def f(eng, out_ap=out_ap, in_ap=in_ap, waits=tuple(waits), sem=sem):
                for (s_, v_) in waits:
                    eng.wait_ge(s_, v_)
                eng.dma_start(out_ap, in_ap, transpose=True).then_inc(sem, 16)
            RY.append(f)
            dcnt[sem] = dcnt.get(sem, 0) + 16
            return dcnt[sem]

        def op(lst, key, sem, builder, waits=(), selfw=True, tick=True):
            myc = cnt[key]
            do_tick = tick
            def f(eng, builder=builder, waits=tuple(waits), myc=myc, sem=sem,
                  selfw=selfw, do_tick=do_tick):
                if selfw and myc:
                    eng.wait_ge(sem, myc)
                for (s_, v_) in waits:
                    eng.wait_ge(s_, v_)
                ins = builder(eng)
                if do_tick:
                    ins.then_inc(sem, 1)
            lst.append(f)
            if do_tick:
                cnt[key] += 1
            return cnt[key]

        def vop(builder, waits=(), selfw=True, tick=True):
            return op(RV, "V", pV, builder, waits, selfw, tick)

        def aop(builder, waits=(), selfw=True, tick=True):
            return op(RA, "A", pA, builder, waits, selfw, tick)

        def top(builder, waits=(), selfw=False, tick=False):
            return op(RT, "T", pT, builder, waits, selfw, tick)

        def tmark(builder, waits=()):
            return op(RT, "T", pT, builder, waits, selfw=True, tick=True)

        c_small = 0
        for (dst, srcap) in ((epsv[:], epsv_in), (id_b[:], identb),
                           (ones_sb[:], ones1), (bd4_sb[:], bdiag4), (s4_sb[:], s4m),
                           (s16_sb[:], s16x2)):
            c_small = dma(RY, sSm, dst, srcap)
        c_wrt = dma(RG, sWr, wrt[:], wrt_l)
        xr = x_in.rearrange("(a p) k -> p a k", p=128)
        c_xcast = []
        for tt in range(n_tt):
            w = [(sXc, 16 * tt)] if tt >= 1 else ()
            c_xcast.append(dma(RG, sXc, xres[:, tt, :], xr[:, tt, :],
                               waits=w, max_dma_last_dim=8192))
        c_xcast_all = dcnt[sXc]

        def emit_ln_sqrt(tt, v_aggr, extra=()):
            pb = tt % 2
            return aop(lambda a, pb=pb: a.activation(
                sd_buf[pb][:], mv_buf[pb][:, 1:2], AF.Sqrt, bias=epsv[:]),
                waits=[(pV, v_aggr)] + list(extra))

        m2d = mask_in
        c_mload, v_red, t_bd, v_gt, t_piece = [], [], [], [], []
        ln1_xb = [None] * n_tt
        ln1_ap = [None] * n_tt
        ln1_aggr = [None] * n_tt
        ln1_sqrt = [None] * n_tt
        n_mloc = D // 8 // 128
        for i in range(n_mloc):
            qs = (i % 2 == 0)
            lst = RY if qs else RA
            sem = sM0 if qs else sM1
            w = []
            if i >= 2:
                w.append((pV, v_red[i - 2]))
            c_mload.append(dma(lst, sem, mview[i % 2],
                               m2d[128 * i:128 * (i + 1), :], waits=w))
            def _red(v, i=i):
                with nc.allow_low_precision(reason="mask block sums <=32, exact in bf16"):
                    return v.tensor_reduce(
                        csw[:, NB * i:NB * (i + 1)],
                        mview[i % 2].rearrange("p (a c) -> p a c", c=32),
                        axis=AX.X, op=ALU.add)
            v_red.append(vop(_red, waits=[(sem, c_mload[i])]))
            bank = pp[i % 2]
            w = [(pV, v_red[i])]
            if i == 0:
                w.append((sSm, c_small))
            t_bd.append(tmark(
                lambda t, i=i, bank=bank: t.matmul(
                    bank[0:4, 0:NB], bd4_sb[:], csw[:, NB * i:NB * (i + 1)],
                    start=True, stop=True, skip_group_check=True),
                waits=w))
            v_gt.append(vop(
                lambda v, i=i, bank=bank: v.tensor_single_scalar(
                    bm4w[:, NB * i:NB * (i + 1)], bank[0:4, 0:NB], 0.0, op=ALU.is_gt),
                waits=[(pT, t_bd[i])]))
        c_bmout = dma(RY, sCc, bmg_in, bm4w[:, 0:NB * n_mloc],
                      waits=[(pV, v_gt[-1])])
        def _cc(eng):
            eng.wait_ge(sCc, c_bmout)
            eng.collective_compute(
                "AllGather", ALU.bypass,
                replica_groups=[list(range(N_CORES))],
                ins=[bmg_in.opt()], outs=[bmg_out.opt()]).then_inc(sCol, 1)
        RG.append(_cc)
        c_bmin = dma(RY, sCc, bm4w[:].rearrange("j (r di kb) -> j r di kb",
                                                r=N_CORES, di=n_mloc),
                     bmg_out[:].rearrange("(r j) (di kb) -> j r di kb",
                                          j=4, kb=NB),
                     waits=[(sCol, 1)])
        c_bm_ready = c_bmin

        n_mtile = D // 128
        for i in range(n_mtile):
            t_piece.append(tmark(
                lambda t, i=i: t.matmul(
                    pp[2][:, 4 * i:4 * (i + 1)], bm4w[:, NB * i:NB * (i + 1)],
                    id_b[0:4, 0:4], start=(i == 0), stop=(i == n_mtile - 1),
                    skip_group_check=True),
                waits=([(sCc, c_bm_ready)] if i == 0 else ())))

        junkA = r2[:, D:2 * D]
        v_nm = [None] * n_tt
        a_ln1 = [None] * n_tt

        def emit_ln1_stats(tt):
            pb = tt % 2
            if tt % 2 == 0:
                wbn = [(sXc, c_xcast[tt])]
                for j in range(D // 512):
                    vop(lambda v, tt=tt, j=j, pb=pb: v.bn_stats(
                        bn_buf[pb][:, 6 * j:6 * (j + 1)],
                        xres[:, tt, 512 * j:512 * (j + 1)]),
                        waits=(wbn if j == 0 else ()), selfw=(j == 0),
                        tick=(j == D // 512 - 1))
                ln1_aggr[tt] = vop(lambda v, pb=pb: v.bn_aggr(
                    mv_buf[pb][:], bn_buf[pb][:]))
                return
            w = [(sXc, c_xcast[tt])]
            if tt >= 2:
                w.append((pV, v_nm[tt - 2]))
            aop(lambda a, tt=tt, pb=pb: a.activation(
                junkA, xres[:, tt, :], AF.Copy, accum_out=mv_buf[pb][:, 0:1]),
                waits=w)
            a_ln1[tt] = aop(lambda a, tt=tt, pb=pb: a.activation(
                junkA, xres[:, tt, :], AF.Square, accum_out=mv_buf[pb][:, 1:2]),
                waits=[(pV, v_nm[tt - 2])] if tt >= 2 else ())
            vop(lambda v, pb=pb: v.tensor_single_scalar(
                mv_buf[pb][:, 0:1], mv_buf[pb][:, 0:1], 1.0 / D, op=ALU.mult),
                waits=[(pA, a_ln1[tt])])
            vop(lambda v, pb=pb: v.tensor_tensor(
                sd_buf[pb][:], mv_buf[pb][:, 0:1], mv_buf[pb][:, 0:1],
                op=ALU.mult))
            ln1_aggr[tt] = vop(lambda v, pb=pb: v.scalar_tensor_tensor(
                mv_buf[pb][:, 1:2], mv_buf[pb][:, 1:2], 1.0 / D, sd_buf[pb][:],
                op0=ALU.mult, op1=ALU.subtract))

        emit_ln1_stats(0)
        if n_tt > 1:
            emit_ln1_stats(1)
        for tt in range(n_tt):
            ln1_sqrt[tt] = emit_ln_sqrt(tt, ln1_aggr[tt],
                                        extra=([(sSm, c_small)] if tt == 0 else ()))
            pb = tt % 2
            vop(lambda v, pb=pb: v.reciprocal(ri_buf[pb][:], sd_buf[pb][:]),
                waits=[(pA, ln1_sqrt[tt])])
            v_nm[tt] = vop(lambda v, pb=pb: v.tensor_scalar(
                nm_buf[pb][:], mv_buf[pb][:, 0:1], ri_buf[pb][:], -1.0,
                op0=ALU.mult, op1=ALU.mult))
            wap = []
            if n_tt < 8:
                if tt >= 1:
                    wap.append(((sB0 if (tt - 1) % 2 == 0 else sB1), ln1_xb[tt - 1]))
            elif tt >= 2:
                wap.append(((sB0 if tt % 2 == 0 else sB1), ln1_xb[tt - 2]))
            if tt == 7 and n_tt >= 8:
                wap.append((sB0, ln1_xb[6]))
            ln1_ap[tt] = vop(lambda v, tt=tt, pb=pb: v.tensor_scalar(
                ln1_src(tt), xres[:, tt, :], ri_buf[pb][:], nm_buf[pb][:],
                op0=ALU.mult, op1=ALU.add), waits=wap)
            w = [(pV, ln1_ap[tt])]
            ln1_xb[tt] = xbar((sB0 if tt % 2 == 0 else sB1),
                              xnt[:, tt, :, :], ln1_src(tt), waits=w)
            if tt + 2 < n_tt:
                emit_ln1_stats(tt + 2)
        ln1_b0, ln1_b1 = dcnt.get(sB0, 0), dcnt.get(sB1, 0)

        v_bmt = vop(lambda v: v.tensor_copy(bmt_sb[:], pp[2][:, 0:NB]),
                    waits=[(pT, t_piece[-1])])
        c_asm1 = dma(RY, sAsm, bmscr, bmt_sb[:], waits=[(pV, v_bmt)])
        c_asm = dma(RY, sAsm, bmTw[:].rearrange("k (kt j) -> k kt j", j=NB),
                    bmscr.rearrange("(kt k) j -> k kt j", k=4),
                    waits=[(sAsm, c_asm1)])
        t_mex, v_mex = [], []
        for c in range(NKT // 4):
            bank = pp[3 + (c % 2)]
            w = [(sAsm, c_asm)] if c == 0 else ()
            if c >= 2:
                w = list(w) + [(pV, v_mex[c - 2])]
            t_mex.append(tmark(
                lambda t, c=c, bank=bank: t.matmul(
                    bank[:], s4_sb[:], bmTw[:, 512 * c:512 * (c + 1)],
                    start=True, stop=True, skip_group_check=True),
                waits=w))
            v_mex.append(vop(
                lambda v, c=c, bank=bank: v.tensor_copy(
                    mexp1[:, 4 * c:4 * (c + 1), :], bank[:]),
                waits=[(pT, t_mex[c])]))
        v_mask_done = v_mex[-1]

        if debug:
            wdm = [(pV, cnt["V"]), (pT, cnt["T"]),
                   (sB0, ln1_b0), (sB1, ln1_b1)]
            c_dm = dma(RG, sAt, d_bm4w, bm4w, waits=wdm)
            c_dm = dma(RG, sAt, d_bmt, bmt_sb[:], waits=wdm)
            c_dm = dma(RG, sAt, d_bmtw, bmTw, waits=wdm)
            c_dm = dma(RG, sAt, d_csw, csw[:, 0:2048], waits=wdm)
            c_dm = dma(RG, sAt, d_mex, mexp1[:].rearrange("p a c -> p (a c)"),
                       waits=[(pV, cnt["V"])])
            c_dm = dma(RG, sAt, d_xnt, xnt[:].rearrange("p a b c -> p (a b c)"),
                       waits=[(sB0, ln1_b0), (sB1, ln1_b1)], max_dma_last_dim=4096)
            dbg_pre_attn = c_dm

        t_slab = []
        v_evac = []
        c_wcast = []

        def emit_attn_apply(s):
            b = s % 2
            w = [((sW0 if b == 0 else sW1), c_wcast[s])]
            if s < 2:
                w.append((pV, v_mask_done))
            else:
                w.append((pT, t_slab[s - 2]))
            return vop(lambda v, b=b, s=s: v.tensor_tensor(
                wts[:, b].rearrange("p a (j r) -> p a j r", r=32),
                wts[:, b].rearrange("p a (j r) -> p a j r", r=32),
                mexp1[:, :, nbs * s:nbs * (s + 1)].unsqueeze(3)
                    .broadcast_to([128, NKT, nbs, 32]),
                op=ALU.mult), waits=w)

        def emit_attn_evac(s):
            b = s % 2
            last = None
            for tt in range(n_tt):
                bank = pp[4 * b + tt // 2]
                c0 = 256 * (tt % 2)
                last = vop(lambda v, tt=tt, s=s, bank=bank, c0=c0: v.tensor_tensor(
                    xres[:, tt, slab_n * s:slab_n * (s + 1)],
                    bank[:, c0:c0 + slab_n],
                    xres[:, tt, slab_n * s:slab_n * (s + 1)], op=ALU.add),
                    waits=([(pT, t_slab[s])] if tt == 0 else ()),
                    selfw=(tt == 0), tick=(tt == n_tt - 1))
            v_evac.append(last)

        v_applies = []
        for s in range(n_slab):
            b = s % 2
            if s < 2:
                w = [(pV, v_red[n_mloc - 2 + s]), (sXc, c_xcast[n_tt - 1])]
                if debug:
                    w.append((sAt, dbg_pre_attn))
            else:
                w = [(pT, t_slab[s - 2])]
            c_wcast.append(dma(RG, (sW0 if b == 0 else sW1),
                               wts[:, b, :, :].rearrange("p a c -> p (a c)"),
                               wta[s], waits=w, max_dma_last_dim=8192))
            v_applies.append(emit_attn_apply(s))
            if s >= 1:
                pass
            w = [(pV, v_applies[s])]
            if s == 0:
                w.append((sB0, ln1_b0))
                w.append((sB1, ln1_b1))
            if s >= 2:
                w.append((pV, v_evac[s - 2]))
            first = True
            for kt in range(NKT):
                for tt in range(n_tt):
                    bank = pp[4 * b + tt // 2]
                    c0 = 256 * (tt % 2)
                    st = (kt == 0 and tt % 2 == 0)
                    mark = (kt == NKT - 1 and tt == n_tt - 1)
                    f = (lambda t, kt=kt, tt=tt, b=b, bank=bank, c0=c0, st=st: t.matmul(
                        bank[:, c0:c0 + slab_n],
                        xnt[:, tt, kt, :], wts[:, b, kt, :],
                        start=st, stop=(kt == NKT - 1),
                        skip_group_check=True))
                    if mark:
                        t_slab.append(tmark(f, waits=(w if first else ())))
                    else:
                        top(f, waits=(w if first else ()))
                    first = False
            if debug and s == 0:
                for tt in range(n_tt):
                    bank = pp[4 * b + tt // 2]
                    c0 = 256 * (tt % 2)
                    vop(lambda v, tt=tt, bank=bank, c0=c0: v.tensor_copy(
                        ost[:, 4 + tt, :], bank[:, c0:c0 + slab_n]),
                        waits=[(pT, t_slab[0])])
                v_datt = cnt["V"]
                c_datt = dma(RG, sAt, d_att[:, 0:n_tt * slab_n],
                             ost[:, 4:4 + n_tt, :], waits=[(pV, v_datt)])
            if s >= 1:
                emit_attn_evac(s - 1)
                if (s - 1) % 2 == 1 and slab_n == 256:
                    jj = (s - 1) // 2
                    for tt in range(n_tt):
                        vop(lambda v, tt=tt, jj=jj: v.bn_stats(
                            bnall[:, 48 * tt + 6 * jj:48 * tt + 6 * (jj + 1)],
                            xres[:, tt, 512 * jj:512 * (jj + 1)]),
                            selfw=(tt == 0), tick=(tt == n_tt - 1))
        emit_attn_evac(n_slab - 1)
        if slab_n == 256:
            jj = (n_slab - 1) // 2
            for tt in range(n_tt):
                vop(lambda v, tt=tt, jj=jj: v.bn_stats(
                    bnall[:, 48 * tt + 6 * jj:48 * tt + 6 * (jj + 1)],
                    xres[:, tt, 512 * jj:512 * (jj + 1)]),
                    selfw=(tt == 0), tick=(tt == n_tt - 1))
        v_attn_done = cnt["V"]
        t_attn_done = t_slab[-1]

        if debug:
            import os as _os
            _gate = int(_os.environ.get("X2GATE", str(n_slab - 1)))
            c_x2e = dma(RG, sAt, d_x2e, xres[:].rearrange("p a c -> p (a c)"),
                        waits=[(pV, v_evac[_gate]), (pT, t_attn_done) if _gate == n_slab - 1 else (pT, t_slab[_gate])],
                        max_dma_last_dim=4096)
            d_xp = nc.dram_tensor("d_xp", [128, n_tt * D], F32, kind="ExternalOutput").ap()
            _w = _os.environ.get("XPW", "ln1")
            if _w == "ln1":
                _wl = [(pV, ln1_ap[-1])]
            elif _w == "xbar":
                _wl = [(sB0, ln1_b0), (sB1, ln1_b1)]
            elif _w == "mask":
                _wl = [(pV, v_mask_done)]
            else:
                _wl = [(pV, v_apply)]
            if _w != "apply":
                dma(RG, sAt, d_xp, xres[:].rearrange("p a c -> p (a c)"),
                    waits=_wl, max_dma_last_dim=4096)
        ln2_ap = [None] * n_tt
        ln2_xb = [None] * n_tt
        ln2_aggr = [None] * n_tt

        c_at = dma(RG, sAt, at3.rearrange("p a c -> p (a c)"), at_l[:],
                   waits=[(pV, v_applies[-1])], max_dma_last_dim=8192)
        c_bp = dma(RG, sAt, bprime, bp_l[:], max_dma_last_dim=8192)
        c_bias = dma(RG, sAt, bias_sb[:, :], bias_in)

        tw = min(4, n_tt)
        nth = n_tt // tw
        nh = tw * 128
        t_rt = None
        t_ht = None

        def emit_router(tt):
            nonlocal t_rt
            sbp = sB0 if tt % 2 == 0 else sB1
            for kc in range(NKT):
                mark = (tt == n_tt - 1 and kc == NKT - 1)
                w = [(sbp, ln2_xb[tt])] if kc == 0 else ()
                if tt == 0 and kc == 0:
                    w = [(sbp, ln2_xb[tt]), (sWr, c_wrt), (pV, v_attn_done)]
                f = (lambda t, tt=tt, kc=kc: t.matmul(
                    pp[0][:, 8 * tt:8 * (tt + 1)], xnt[:, tt, kc, :],
                    wrt[:, 8 * kc:8 * (kc + 1)],
                    start=(tt == 0 and kc == 0), stop=(kc == NKT - 1),
                    skip_group_check=True))
                if mark:
                    t_rt = tmark(f, waits=w)
                else:
                    top(f, waits=w)

        def emit_ht(th):
            nonlocal t_ht
            for kc in range(NKT):
                mark = (th == nth - 1 and kc == NKT - 1)
                w = [(sAt, c_bias)] if (th == 0 and kc == 0) else ()
                f = (lambda t, th=th, kc=kc: t.matmul(
                    pp[1 + th][:, 0:nh], at3[:, kc, :],
                    xnt[:, tw * th:tw * (th + 1), kc, :], start=(kc == 0),
                    stop=(kc == NKT - 1), skip_group_check=True))
                if mark:
                    t_ht = tmark(f, waits=w)
                else:
                    top(f, waits=w)

        for tt in range(n_tt):
            wag = []
            if tt == 0:
                wag = [(pV, v_attn_done)]
                if debug:
                    wag.append((sAt, c_x2e))
            if slab_n == 256:
                ln2_aggr[tt] = vop(lambda v, tt=tt: v.bn_aggr(
                    mvall[:, 2 * tt:2 * (tt + 1)], bnall[:, 48 * tt:48 * (tt + 1)]),
                    waits=wag)
            else:
                pb = tt % 2
                for j in range(D // 512):
                    vop(lambda v, tt=tt, j=j, pb=pb: v.bn_stats(
                        bn_buf[pb][:, 6 * j:6 * (j + 1)],
                        xres[:, tt, 512 * j:512 * (j + 1)]),
                        waits=(wag if j == 0 else ()), selfw=(j == 0),
                        tick=(j == D // 512 - 1))
                ln2_aggr[tt] = vop(lambda v, tt=tt, pb=pb: v.bn_aggr(
                    mvall[:, 2 * tt:2 * (tt + 1)], bn_buf[pb][:]))
            pb = tt % 2
            a_sd = aop(lambda a, tt=tt, pb=pb: a.activation(
                sd_buf[pb][:], mvall[:, 2 * tt + 1:2 * tt + 2], AF.Sqrt, bias=epsv[:]),
                waits=[(pV, ln2_aggr[tt])])
            vop(lambda v, pb=pb: v.reciprocal(ri_buf[pb][:], sd_buf[pb][:]),
                waits=[(pA, a_sd)])
            vop(lambda v, tt=tt, pb=pb: v.tensor_scalar(
                nm_buf[pb][:], mvall[:, 2 * tt:2 * tt + 1], ri_buf[pb][:], -1.0,
                op0=ALU.mult, op1=ALU.mult))
            wap = []
            if tt >= 2:
                wap.append(((sB0 if tt % 2 == 0 else sB1), ln2_xb[tt - 2]))
            ln2_ap[tt] = vop(lambda v, tt=tt, pb=pb: v.tensor_scalar(
                xstg_ln2[pb], xres[:, tt, :], ri_buf[pb][:], nm_buf[pb][:],
                op0=ALU.mult, op1=ALU.add), waits=wap)
            w = [(pV, ln2_ap[tt]), (pT, t_attn_done)]
            ln2_xb[tt] = xbar((sB0 if tt % 2 == 0 else sB1),
                              xnt[:, tt, :, :], xstg_ln2[tt % 2], waits=w)
            emit_router(tt)
            if (tt + 1) % tw == 0:
                emit_ht((tt + 1) // tw - 1)
        ln2_b0, ln2_b1 = dcnt.get(sB0, 0), dcnt.get(sB1, 0)

        v_lg = vop(lambda v: v.tensor_copy(logits[:, 0:8 * n_tt], pp[0][:, 0:8 * n_tt]),
                   waits=[(pT, t_rt)])
        v_cmb = []
        for tt in range(n_tt):
            lg = logits[:, 8 * tt:8 * (tt + 1)]
            vop(lambda v, lg=lg: v.reduce_max(m1_t[:], lg, axis=AX.X))
            vop(lambda v, lg=lg: v.tensor_scalar(eq1_t[:], lg, m1_t[:], None,
                                                 op0=ALU.is_equal))
            vop(lambda v, lg=lg: v.scalar_tensor_tensor(l2_t[:], eq1_t[:], -1e30, lg,
                                                        op0=ALU.mult, op1=ALU.add))
            vop(lambda v: v.reduce_max(m2_t[:], l2_t[:], axis=AX.X))
            vop(lambda v: v.tensor_scalar(eq2_t[:], l2_t[:], m2_t[:], None,
                                          op0=ALU.is_equal))
            vop(lambda v: v.tensor_tensor(eqs_t[:], eq1_t[:], eq2_t[:], op=ALU.add))
            v_nm = vop(lambda v: v.tensor_scalar(nm1_t[:], m1_t[:], -1.0, None,
                                                 op0=ALU.mult))
            a_e = aop(lambda a, lg=lg: a.activation(ee_t[:], lg, AF.Exp, bias=nm1_t[:]),
                      waits=[(pV, v_nm)])
            vop(lambda v: v.tensor_tensor(cc_t[:], ee_t[:], eqs_t[:], op=ALU.mult),
                waits=[(pA, a_e)])
            vop(lambda v: v.reduce_sum(ssum_t[:], cc_t[:], axis=AX.X))
            vop(lambda v: v.reciprocal(rs_t[:], ssum_t[:]))
            v_cmb.append(vop(lambda v, tt=tt: v.tensor_scalar(
                combine[:, 8 * tt:8 * (tt + 1)], cc_t[:], rs_t[:], None, op0=ALU.mult)))
        ct_bank = [pp[3], pp[0]]
        t_ct = []
        for tt in range(n_tt):
            bank = ct_bank[tt // tw]
            w = [(pV, v_cmb[tt])]
            if tt == tw:
                w.append((pV, v_lg))
            t_ct.append(tmark(lambda t, tt=tt, bank=bank: t.matmul(
                bank[0:8, 128 * (tt % tw):128 * (tt % tw + 1)],
                combine[:, 8 * tt:8 * (tt + 1)], id_b[:],
                start=(tt % tw == 0), stop=(tt % tw == tw - 1 or tt == n_tt - 1),
                skip_group_check=True), waits=w))
        v_ct = []
        for h in range(nth):
            v_ct.append(vop(lambda v, h=h: v.tensor_copy(
                combT[0:8, nh * h:nh * (h + 1)], ct_bank[h][0:8, 0:nh]),
                waits=[(pT, t_ct[min(tw * h + tw - 1, n_tt - 1)])]))
        t_cte, v_cte = [], []
        for th in range(nth):
            w = [(pV, v_ct[th])]
            if th >= 1:
                w.append((pV, v_cte[th - 1]))
            t_cte.append(tmark(lambda t, th=th: t.matmul(
                pp[3][:, 0:nh], s16_sb[:], combT[0:8, nh * th:nh * (th + 1)],
                start=True, stop=True, skip_group_check=True),
                waits=w))
            v_cte.append(vop(lambda v, th=th: v.tensor_copy(
                hwt[:, nh * th:nh * (th + 1)], pp[3][:, 0:nh]),
                waits=[(pT, t_cte[th])]))
        v_hwt = []
        for th in range(nth):
            v_hwt.append(vop(lambda v, th=th: v.tensor_tensor(
                hwt[:, nh * th:nh * (th + 1)], pp[1 + th][:, 0:nh],
                hwt[:, nh * th:nh * (th + 1)], op=ALU.mult),
                waits=[(pT, t_ht), (pV, v_cte[th])]))
        v_prep_done = v_hwt[-1]

        t_fslab, v_fevac, c_store = [], [], []
        y2 = y_out.rearrange("(a p) k -> p a k", p=128)
        for s in range(n_slab):
            b = s % 2
            w = [(sB0, ln2_b0), (sB1, ln2_b1), (sAt, c_bias)]
            if s >= 2:
                w.append((pT, t_fslab[s - 2]))
            else:
                w.append((pT, t_attn_done))
            c_w = dma(RG, (sW0 if b == 0 else sW1),
                      wts[:, b, :, :].rearrange("p a c -> p (a c)"),
                      wtb[s], waits=w, max_dma_last_dim=8192)
            c_wcast.append(c_w)
            bps = (s + 1) % 2
            w = [((sW0 if b == 0 else sW1), c_w)]
            if s >= 2:
                w.append((pV, v_fevac[s - 2]))
            elif s == 0:
                w.append((pV, v_attn_done))
            else:
                w.append((pV, v_prep_done))
            first = True
            for kt in range(NKT):
                for tt in range(n_tt):
                    bank = pp[4 * bps + tt // 2]
                    top(lambda t, kt=kt, tt=tt, b=b, bank=bank: t.matmul(
                        bank[:, 256 * (tt % 2):256 * (tt % 2) + slab_n],
                        xnt[:, tt, kt, :], wts[:, b, kt, :],
                        start=(kt == 0 and tt % 2 == 0), stop=False,
                        skip_group_check=True), waits=(w if first else ()))
                    first = False
            for tt in range(n_tt):
                bank = pp[4 * bps + tt // 2]
                wl = [(pV, v_prep_done)] if (s == 0 and tt == 0) else ()
                top(lambda t, tt=tt, s=s, bank=bank: t.matmul(
                    bank[:, 256 * (tt % 2):256 * (tt % 2) + slab_n],
                    hwt[:, 128 * tt:128 * (tt + 1)],
                    bprime[:, slab_n * s:slab_n * (s + 1)],
                    start=False, stop=False, skip_group_check=True), waits=wl)
                mark = (tt == n_tt - 1)
                f = (lambda t, tt=tt, s=s, bank=bank: t.matmul(
                    bank[:, 256 * (tt % 2):256 * (tt % 2) + slab_n],
                    ones_sb[:], bias_sb[:, slab_n * s:slab_n * (s + 1)],
                    start=False, stop=True, skip_group_check=True))
                if mark:
                    t_fslab.append(tmark(f))
                else:
                    top(f)
            last = None
            for tt in range(n_tt):
                bank = pp[4 * bps + tt // 2]
                w2 = []
                if tt == 0:
                    w2 = [(pT, t_fslab[s])]
                    if s >= 1:
                        w2.append((sSt, c_store[s - 1]))
                last = vop(lambda v, tt=tt, s=s, bank=bank: v.tensor_tensor(
                    ost[:, tt, :], bank[:, 256 * (tt % 2):256 * (tt % 2) + slab_n],
                    xres[:, tt, slab_n * s:slab_n * (s + 1)], op=ALU.add),
                    waits=w2, selfw=(tt == 0), tick=(tt == n_tt - 1))
            v_fevac.append(last)
            c_store.append(dma(RY, sSt,
                               y2[:, :, slab_n * s:slab_n * (s + 1)], ost[:, 0:n_tt, :],
                               waits=[(pV, v_fevac[s])]))
        c_final = dcnt[sSt]
        if debug:
            wfin = [(sSt, c_final), (pV, cnt["V"]), (pA, cnt["A"]), (pT, cnt["T"])]
            dma(RG, sAt, d_xres, xres[:].rearrange("p a c -> p (a c)"),
                waits=wfin, max_dma_last_dim=4096)
            dma(RG, sAt, d_comb, combine[:], waits=wfin)
            dma(RG, sAt, d_hwt, hwt[:], waits=wfin)
            c_final2 = dcnt[sAt]

        with nc.Block() as blk:
            @blk.gpsimd
            def _(g):
                for f in RG:
                    f(g)

            @blk.sync
            def _(s):
                for f in RY:
                    f(s)
                s.wait_ge(sSt, c_final)
                if debug:
                    s.wait_ge(sAt, c_final2)

            @blk.vector
            def _(v):
                for f in RV:
                    f(v)

            @blk.scalar
            def _(a):
                for f in RA:
                    f(a)

            @blk.tensor
            def _(t):
                for f in RT:
                    f(t)

    return nc


def _lay_weights(attn_w, base_w, a_w, b_w, wr_w, n_slab=16, slab_n=256):
    wt = np.ascontiguousarray(
        attn_w.T.reshape(NKT, 128, n_slab, slab_n).transpose(2, 1, 0, 3)
    ).reshape(n_slab, 128, NKT * slab_n)
    wtb = np.ascontiguousarray(
        base_w.T.reshape(NKT, 128, n_slab, slab_n).transpose(2, 1, 0, 3)
    ).reshape(n_slab, 128, NKT * slab_n)
    at = np.ascontiguousarray(
        a_w.reshape(128, NKT, 128).transpose(2, 1, 0)).reshape(128, D)
    bp = np.ascontiguousarray(b_w.transpose(0, 2, 1)).reshape(128, D)
    wrt = np.ascontiguousarray(
        wr_w.reshape(8, NKT, 128).transpose(2, 1, 0)).reshape(128, NKT * 8)
    return wt, wtb, at, bp, wrt


def _consts():
    p = np.arange(128)
    identb = np.eye(128, dtype=np.float32).astype(ml_dtypes.bfloat16)
    identf = None
    ones1 = np.ones((1, 128), dtype=ml_dtypes.bfloat16)
    bdiag4 = (p[:, None] // 32 == np.arange(4)[None, :]).astype(np.float32).astype(ml_dtypes.bfloat16)
    s4m = (np.arange(4)[:, None] == p[None, :] // 32).astype(np.float32).astype(ml_dtypes.bfloat16)
    s16 = (2.0 * (np.arange(8)[:, None] == p[None, :] // 16)).astype(np.float32).astype(ml_dtypes.bfloat16)
    epsv = np.full((128, 1), 1e-5, dtype=np.float32)
    return identb, identf, ones1, bdiag4, s4m, s16, epsv


def kernel(L0_attn_data, L0_attn_weight, L0_attn_mask, L0_ffn_W_base,
           L0_ffn_b_base, L0_ffn_W_router, L0_ffn_A, L0_ffn_B):
    x2d = np.ascontiguousarray(np.asarray(L0_attn_data, dtype=np.float32).reshape(8192, D))
    wt, wtb, at, bp, wrt = _lay_weights(
        np.asarray(L0_attn_weight, dtype=np.float32),
        np.asarray(L0_ffn_W_base, dtype=np.float32),
        np.asarray(L0_ffn_A, dtype=np.float32).reshape(128, D),
        np.asarray(L0_ffn_B, dtype=np.float32),
        np.asarray(L0_ffn_W_router, dtype=np.float32))
    mask = np.asarray(L0_attn_mask, dtype=np.int32)
    bias = np.asarray(L0_ffn_b_base, dtype=np.float32).reshape(1, D)
    identb, identf, ones1, bdiag4, s4m, s16, epsv = _consts()

    nc = build_core_program()
    shared = dict(wta=wt, wtb=wtb, at_l=at, bp_l=bp, wrt_l=wrt,
                  bias_in=bias, epsv_in=epsv, identb=identb,
                  ones1=ones1, bdiag4=bdiag4, s4m=s4m, s16x2=s16)
    in_maps = []
    for c in range(N_CORES):
        m = dict(shared)
        m["x_in"] = np.ascontiguousarray(x2d[1024 * c:1024 * (c + 1), :])
        m["mask_in"] = np.ascontiguousarray(mask[512 * c:512 * (c + 1), :])
        in_maps.append(m)
    res = run_bass_kernel_spmd(nc, in_maps, core_ids=list(range(N_CORES)))
    out = np.concatenate([r["y_out"] for r in res.results], axis=0)
    return out.reshape(2, 4096, 4096)


# version 22
# speedup vs baseline: 1.0527x; 1.0306x over previous
import numpy as np
import ml_dtypes
import concourse.bass as bass
import concourse.mybir as mybir
from concourse.bass_utils import run_bass_kernel_spmd

F32 = mybir.dt.float32
BF16 = mybir.dt.bfloat16
I32 = mybir.dt.int32
AF = mybir.ActivationFunctionType
ALU = mybir.AluOpType
AX = mybir.AxisListType

D = 4096
NKT = 32
NB = 128
N_CORES = 8


def build_core_program(n_tt=8, n_slab=16, slab_n=256, debug=False, sep_banks=False, skipmask=False):
    T = n_tt * 128
    assert n_slab * slab_n == D
    nbs = slab_n // 32

    import os as _os0
    _scr = int(_os0.environ.get("DMASCRATCH", "16384"))
    nc = bass.Bass("TRN2", target_bir_lowering=False, dynamic_dma_scratch_size=_scr)
    x_in = nc.dram_tensor("x_in", [T, D], F32, kind="ExternalInput").ap()
    wta = nc.dram_tensor("wta", [n_slab, 128, NKT * slab_n], F32, kind="ExternalInput").ap()
    wtb = nc.dram_tensor("wtb", [n_slab, 128, NKT * slab_n], F32, kind="ExternalInput").ap()
    mask_in = nc.dram_tensor("mask_in", [D // 8, D], I32, kind="ExternalInput").ap()
    bmg_in = nc.dram_tensor("bmg_in", [4, D // 8], BF16, kind="Internal").ap()
    bmg_out = nc.dram_tensor("bmg_out", [32, D // 8], BF16, kind="Internal").ap()
    bmscr = nc.dram_tensor("bmscr", [128, 128], BF16, kind="Internal").ap()
    at_l = nc.dram_tensor("at_l", [128, D], F32, kind="ExternalInput").ap()
    bp_l = nc.dram_tensor("bp_l", [128, D], F32, kind="ExternalInput").ap()
    wrt_l = nc.dram_tensor("wrt_l", [128, 256], F32, kind="ExternalInput").ap()
    bias_in = nc.dram_tensor("bias_in", [1, D], F32, kind="ExternalInput").ap()
    epsv_in = nc.dram_tensor("epsv_in", [128, 1], F32, kind="ExternalInput").ap()
    identb = nc.dram_tensor("identb", [128, 128], BF16, kind="ExternalInput").ap()
    ones1 = nc.dram_tensor("ones1", [1, 128], BF16, kind="ExternalInput").ap()
    bdiag4 = nc.dram_tensor("bdiag4", [128, 4], BF16, kind="ExternalInput").ap()
    s4m = nc.dram_tensor("s4m", [4, 128], BF16, kind="ExternalInput").ap()
    s16x2 = nc.dram_tensor("s16x2", [8, 128], BF16, kind="ExternalInput").ap()
    y_out = nc.dram_tensor("y_out", [T, D], F32, kind="ExternalOutput").ap()
    if debug:
        d_xres = nc.dram_tensor("d_xres", [128, n_tt * D], F32, kind="ExternalOutput").ap()
        d_stats = nc.dram_tensor("d_stats", [128, 2 * n_tt], F32, kind="ExternalOutput").ap()
        d_comb = nc.dram_tensor("d_comb", [128, 8 * n_tt], F32, kind="ExternalOutput").ap()
        d_mex = nc.dram_tensor("d_mex", [128, D], F32, kind="ExternalOutput").ap()
        d_hwt = nc.dram_tensor("d_hwt", [128, T], F32, kind="ExternalOutput").ap()
        d_xnt = nc.dram_tensor("d_xnt", [128, n_tt * NKT * 128], F32, kind="ExternalOutput").ap()
        d_csw = nc.dram_tensor("d_csw", [128, 2048], F32, kind="ExternalOutput").ap()
        d_bm4w = nc.dram_tensor("d_bm4w", [4, D], F32, kind="ExternalOutput").ap()
        d_bmt = nc.dram_tensor("d_bmt", [128, NB], F32, kind="ExternalOutput").ap()
        d_bmtw = nc.dram_tensor("d_bmtw", [4, D], F32, kind="ExternalOutput").ap()
        d_att = nc.dram_tensor("d_att", [128, 1024], F32, kind="ExternalOutput").ap()
        d_x2e = nc.dram_tensor("d_x2e", [128, n_tt * D], F32, kind="ExternalOutput").ap()

    from contextlib import ExitStack
    with ExitStack() as ctx:
        e = ctx.enter_context
        xres = e(nc.sbuf_tensor([128, n_tt, D], BF16))
        xnt = e(nc.sbuf_tensor([128, n_tt, NKT, 128], BF16))
        wts = e(nc.sbuf_tensor([128, 2, NKT, slab_n], BF16))
        r2 = e(nc.sbuf_tensor([128, 2 * D], BF16))
        xstg = e(nc.sbuf_tensor([128, D], BF16))
        ost = e(nc.sbuf_tensor([128, max(n_tt, 8), slab_n], F32))
        hwt = e(nc.sbuf_tensor([128, T], BF16))
        wrt = e(nc.sbuf_tensor([128, 256], BF16))

        logits = e(nc.sbuf_tensor([128, 8 * n_tt], F32))
        combine = e(nc.sbuf_tensor([128, 8 * n_tt], BF16))
        packed = e(nc.sbuf_tensor([128, D], BF16))
        bmt_sb = e(nc.sbuf_tensor([128, NB], BF16))
        s1s = e(nc.sbuf_tensor([128, n_tt], F32))
        s2s = e(nc.sbuf_tensor([128, n_tt], F32))
        scr = e(nc.sbuf_tensor([128, 192], F32))
        bnall = e(nc.sbuf_tensor([128, 8 * 6 * 8], F32))
        mvall = e(nc.sbuf_tensor([128, 16], F32))
        mean_t = scr[:, 0:1]; ex2_t = scr[:, 1:2]; msq_t = scr[:, 2:3]
        var_t = scr[:, 3:4]; sd_t = scr[:, 4:5]; rinv_t = scr[:, 5:6]
        negmr_t = scr[:, 6:7]; epsv = scr[:, 7:8]; m1_t = scr[:, 8:9]
        m2_t = scr[:, 9:10]; nm1_t = scr[:, 10:11]; ssum_t = scr[:, 11:12]
        rs_t = scr[:, 12:13]
        eq1_t = scr[:, 16:24]; eq2_t = scr[:, 24:32]; l2_t = scr[:, 32:40]
        eqs_t = scr[:, 40:48]; ee_t = scr[:, 48:56]; cc_t = scr[:, 56:64]
        bn_buf = [scr[:, 64:112], scr[:, 116:164]]
        mv_buf = [scr[:, 112:114], scr[:, 114:116]]
        sd_buf = [scr[:, 164:165], scr[:, 165:166]]
        ri_buf = [scr[:, 166:167], scr[:, 167:168]]
        nm_buf = [scr[:, 168:169], scr[:, 169:170]]
        id_b = e(nc.sbuf_tensor([128, 128], BF16))
        ones_sb = e(nc.sbuf_tensor([1, 128], BF16))
        bd4_sb = e(nc.sbuf_tensor([128, 4], BF16))
        s4_sb = e(nc.sbuf_tensor([4, 128], BF16))
        s16_sb = e(nc.sbuf_tensor([8, 128], BF16))
        pp = [e(nc.psum_tensor(f"pp{i}", [128, 512], F32)) for i in range(8)]
        sSm = e(nc.semaphore("sSm"))
        sXc = e(nc.semaphore("sXc"))
        sW0 = e(nc.semaphore("sW0"))
        sW1 = e(nc.semaphore("sW1"))
        sM0 = e(nc.semaphore("sM0"))
        sM1 = e(nc.semaphore("sM1"))
        sB0 = e(nc.semaphore("sB0"))
        sB1 = e(nc.semaphore("sB1"))
        sAsm = e(nc.semaphore("sAsm"))
        sAsm2 = e(nc.semaphore("sAsm2"))
        sSt = e(nc.semaphore("sSt"))
        sAt = e(nc.semaphore("sAt"))
        sWr = e(nc.semaphore("sWr"))
        sCc = e(nc.semaphore("sCc"))
        sCol = e(nc.semaphore("sCol"))
        pV = e(nc.semaphore("pV"))
        pA = e(nc.semaphore("pA"))
        pT = e(nc.semaphore("pT"))

        mview = [wts[:, b, :, :].rearrange("p a c -> p (a c)").bitcast(I32)
                 for b in range(2)]
        mexp1 = r2[:, 0:D].rearrange("p (a c) -> p a c", a=NKT)
        at3 = r2[:, 0:D].rearrange("p (a c) -> p a c", a=NKT)
        bprime = r2[:, D:2 * D]
        csw = ost[:].rearrange("p a c -> p (a c)").bitcast(BF16)
        junk = [xstg[:], xstg[:]]
        xnt7 = xnt[:, n_tt - 1].rearrange("p a b -> p (a b)") if n_tt >= 8 else None
        def ln1_src(tt):
            if n_tt < 8 or tt % 2 == 0 or tt == 7:
                return xstg[:]
            return xnt7
        xstg_ln2 = [xstg[:], ost[:].rearrange("p a c -> p (a c)").bitcast(BF16)[:, 0:D]]
        ln2_stage = wts[:, 0].rearrange("p a c -> p (a c)")
        bm4w = packed[0:4, 0:D]
        bmTw = packed[0:4, 0:D]
        bias_sb = packed[0:1, 0:D]
        combT = ost[:].rearrange("p a c -> p (a c)").bitcast(BF16)[0:8, 0:T]

        RG, RY, RV, RA, RT = [], [], [], [], []
        cnt = {"V": 0, "A": 0, "T": 0}
        dcnt = {}

        def dma(lst, sem, out_ap, in_ap, waits=(), **kw):
            def f(eng, out_ap=out_ap, in_ap=in_ap, kw=kw, waits=tuple(waits), sem=sem):
                for (s_, v_) in waits:
                    eng.wait_ge(s_, v_)
                eng.dma_start(out_ap, in_ap, **kw).then_inc(sem, 16)
            lst.append(f)
            dcnt[sem] = dcnt.get(sem, 0) + 16
            return dcnt[sem]

        def xbar(sem, out_ap, in_ap, waits=()):
            def f(eng, out_ap=out_ap, in_ap=in_ap, waits=tuple(waits), sem=sem):
                for (s_, v_) in waits:
                    eng.wait_ge(s_, v_)
                eng.dma_start(out_ap, in_ap, transpose=True).then_inc(sem, 16)
            RY.append(f)
            dcnt[sem] = dcnt.get(sem, 0) + 16
            return dcnt[sem]

        def op(lst, key, sem, builder, waits=(), selfw=True, tick=True):
            myc = cnt[key]
            do_tick = tick
            def f(eng, builder=builder, waits=tuple(waits), myc=myc, sem=sem,
                  selfw=selfw, do_tick=do_tick):
                if selfw and myc:
                    eng.wait_ge(sem, myc)
                for (s_, v_) in waits:
                    eng.wait_ge(s_, v_)
                ins = builder(eng)
                if do_tick:
                    ins.then_inc(sem, 1)
            lst.append(f)
            if do_tick:
                cnt[key] += 1
            return cnt[key]

        def vop(builder, waits=(), selfw=True, tick=True):
            return op(RV, "V", pV, builder, waits, selfw, tick)

        def aop(builder, waits=(), selfw=True, tick=True):
            return op(RA, "A", pA, builder, waits, selfw, tick)

        def top(builder, waits=(), selfw=False, tick=False):
            return op(RT, "T", pT, builder, waits, selfw, tick)

        def tmark(builder, waits=()):
            return op(RT, "T", pT, builder, waits, selfw=True, tick=True)

        c_small = 0
        for (dst, srcap) in ((epsv[:], epsv_in), (id_b[:], identb),
                           (ones_sb[:], ones1), (bd4_sb[:], bdiag4), (s4_sb[:], s4m),
                           (s16_sb[:], s16x2)):
            c_small = dma(RY, sSm, dst, srcap)
        c_wrt = dma(RG, sWr, wrt[:], wrt_l)
        xr = x_in.rearrange("(a p) k -> p a k", p=128)
        c_xcast = []
        rg_cc_idx = None
        for tt in range(n_tt):
            w = [(sXc, 16 * tt)] if tt >= 1 else ()
            c_xcast.append(dma(RG, sXc, xres[:, tt, :], xr[:, tt, :],
                               waits=w, max_dma_last_dim=8192))
            if tt == min(2, n_tt - 1):
                rg_cc_idx = len(RG)
        c_xcast_all = dcnt[sXc]

        def emit_ln_sqrt(tt, v_aggr, extra=()):
            pb = tt % 2
            return aop(lambda a, pb=pb: a.activation(
                sd_buf[pb][:], mv_buf[pb][:, 1:2], AF.Sqrt, bias=epsv[:]),
                waits=[(pV, v_aggr)] + list(extra))

        m2d = mask_in
        c_mload, v_red, t_bd, v_gt, t_piece = [], [], [], [], []
        ln1_xb = [None] * n_tt
        ln1_ap = [None] * n_tt
        ln1_aggr = [None] * n_tt
        ln1_sqrt = [None] * n_tt
        n_mloc = D // 8 // 128
        for i in range(n_mloc):
            qs = (i % 2 == 0)
            lst = RY if qs else RA
            sem = sM0 if qs else sM1
            w = []
            if i >= 2:
                w.append((pV, v_red[i - 2]))
            c_mload.append(dma(lst, sem, mview[i % 2],
                               m2d[128 * i:128 * (i + 1), :], waits=w))
            def _red(v, i=i):
                with nc.allow_low_precision(reason="mask block sums <=32, exact in bf16"):
                    return v.tensor_reduce(
                        csw[:, NB * i:NB * (i + 1)],
                        mview[i % 2].rearrange("p (a c) -> p a c", c=32),
                        axis=AX.X, op=ALU.add)
            v_red.append(vop(_red, waits=[(sem, c_mload[i])]))
            bank = pp[i % 2]
            w = [(pV, v_red[i])]
            if i == 0:
                w.append((sSm, c_small))
            t_bd.append(tmark(
                lambda t, i=i, bank=bank: t.matmul(
                    bank[0:4, 0:NB], bd4_sb[:], csw[:, NB * i:NB * (i + 1)],
                    start=True, stop=True, skip_group_check=True),
                waits=w))
            v_gt.append(vop(
                lambda v, i=i, bank=bank: v.tensor_single_scalar(
                    bm4w[:, NB * i:NB * (i + 1)], bank[0:4, 0:NB], 0.0, op=ALU.is_gt),
                waits=[(pT, t_bd[i])]))
        c_bmout = dma(RY, sCc, bmg_in, bm4w[:, 0:NB * n_mloc],
                      waits=[(pV, v_gt[-1])])
        def _cc(eng):
            eng.wait_ge(sCc, c_bmout)
            eng.collective_compute(
                "AllGather", ALU.bypass,
                replica_groups=[list(range(N_CORES))],
                ins=[bmg_in.opt()], outs=[bmg_out.opt()]).then_inc(sCol, 1)
        RG.insert(rg_cc_idx, _cc)

        junkA = r2[:, D:2 * D]
        v_nm = [None] * n_tt
        a_ln1 = [None] * n_tt

        def emit_ln1_stats(tt):
            pb = tt % 2
            if tt % 2 == 0:
                wbn = [(sXc, c_xcast[tt])]
                for j in range(D // 512):
                    vop(lambda v, tt=tt, j=j, pb=pb: v.bn_stats(
                        bn_buf[pb][:, 6 * j:6 * (j + 1)],
                        xres[:, tt, 512 * j:512 * (j + 1)]),
                        waits=(wbn if j == 0 else ()), selfw=(j == 0),
                        tick=(j == D // 512 - 1))
                ln1_aggr[tt] = vop(lambda v, pb=pb: v.bn_aggr(
                    mv_buf[pb][:], bn_buf[pb][:]))
                return
            w = [(sXc, c_xcast[tt])]
            if tt >= 2:
                w.append((pV, v_nm[tt - 2]))
            aop(lambda a, tt=tt, pb=pb: a.activation(
                junkA, xres[:, tt, :], AF.Copy, accum_out=mv_buf[pb][:, 0:1]),
                waits=w)
            a_ln1[tt] = aop(lambda a, tt=tt, pb=pb: a.activation(
                junkA, xres[:, tt, :], AF.Square, accum_out=mv_buf[pb][:, 1:2]),
                waits=[(pV, v_nm[tt - 2])] if tt >= 2 else ())
            vop(lambda v, pb=pb: v.tensor_single_scalar(
                mv_buf[pb][:, 0:1], mv_buf[pb][:, 0:1], 1.0 / D, op=ALU.mult),
                waits=[(pA, a_ln1[tt])])
            vop(lambda v, pb=pb: v.tensor_tensor(
                sd_buf[pb][:], mv_buf[pb][:, 0:1], mv_buf[pb][:, 0:1],
                op=ALU.mult))
            ln1_aggr[tt] = vop(lambda v, pb=pb: v.scalar_tensor_tensor(
                mv_buf[pb][:, 1:2], mv_buf[pb][:, 1:2], 1.0 / D, sd_buf[pb][:],
                op0=ALU.mult, op1=ALU.subtract))

        emit_ln1_stats(0)
        if n_tt > 1:
            emit_ln1_stats(1)
        for tt in range(n_tt):
            ln1_sqrt[tt] = emit_ln_sqrt(tt, ln1_aggr[tt],
                                        extra=([(sSm, c_small)] if tt == 0 else ()))
            pb = tt % 2
            vop(lambda v, pb=pb: v.reciprocal(ri_buf[pb][:], sd_buf[pb][:]),
                waits=[(pA, ln1_sqrt[tt])])
            v_nm[tt] = vop(lambda v, pb=pb: v.tensor_scalar(
                nm_buf[pb][:], mv_buf[pb][:, 0:1], ri_buf[pb][:], -1.0,
                op0=ALU.mult, op1=ALU.mult))
            wap = []
            if n_tt < 8:
                if tt >= 1:
                    wap.append(((sB0 if (tt - 1) % 2 == 0 else sB1), ln1_xb[tt - 1]))
            elif tt >= 2:
                wap.append(((sB0 if tt % 2 == 0 else sB1), ln1_xb[tt - 2]))
            if tt == 7 and n_tt >= 8:
                wap.append((sB0, ln1_xb[6]))
            ln1_ap[tt] = vop(lambda v, tt=tt, pb=pb: v.tensor_scalar(
                ln1_src(tt), xres[:, tt, :], ri_buf[pb][:], nm_buf[pb][:],
                op0=ALU.mult, op1=ALU.add), waits=wap)
            w = [(pV, ln1_ap[tt])]
            ln1_xb[tt] = xbar((sB0 if tt % 2 == 0 else sB1),
                              xnt[:, tt, :, :], ln1_src(tt), waits=w)
            if tt + 2 < n_tt:
                emit_ln1_stats(tt + 2)
        ln1_b0, ln1_b1 = dcnt.get(sB0, 0), dcnt.get(sB1, 0)

        c_bmin = dma(RY, sCc, bm4w[:].rearrange("j (r di kb) -> j r di kb",
                                                r=N_CORES, di=n_mloc),
                     bmg_out[:].rearrange("(r j) (di kb) -> j r di kb",
                                          j=4, kb=NB),
                     waits=[(sCol, 1)])
        c_bm_ready = c_bmin

        n_mtile = D // 128
        for i in range(n_mtile):
            t_piece.append(tmark(
                lambda t, i=i: t.matmul(
                    pp[2][:, 4 * i:4 * (i + 1)], bm4w[:, NB * i:NB * (i + 1)],
                    id_b[0:4, 0:4], start=(i == 0), stop=(i == n_mtile - 1),
                    skip_group_check=True),
                waits=([(sCc, c_bm_ready)] if i == 0 else ())))

        v_bmt = vop(lambda v: v.tensor_copy(bmt_sb[:], pp[2][:, 0:NB]),
                    waits=[(pT, t_piece[-1])])
        c_asm1 = dma(RY, sAsm, bmscr, bmt_sb[:], waits=[(pV, v_bmt)])
        c_asm = dma(RY, sAsm, bmTw[:].rearrange("k (kt j) -> k kt j", j=NB),
                    bmscr.rearrange("(kt k) j -> k kt j", k=4),
                    waits=[(sAsm, c_asm1)])
        t_mex, v_mex = [], []
        for c in range(NKT // 4):
            bank = pp[3 + (c % 2)]
            w = [(sAsm, c_asm)] if c == 0 else ()
            if c >= 2:
                w = list(w) + [(pV, v_mex[c - 2])]
            t_mex.append(tmark(
                lambda t, c=c, bank=bank: t.matmul(
                    bank[:], s4_sb[:], bmTw[:, 512 * c:512 * (c + 1)],
                    start=True, stop=True, skip_group_check=True),
                waits=w))
            v_mex.append(vop(
                lambda v, c=c, bank=bank: v.tensor_copy(
                    mexp1[:, 4 * c:4 * (c + 1), :], bank[:]),
                waits=[(pT, t_mex[c])]))
        v_mask_done = v_mex[-1]

        if debug:
            wdm = [(pV, cnt["V"]), (pT, cnt["T"]),
                   (sB0, ln1_b0), (sB1, ln1_b1)]
            c_dm = dma(RG, sAt, d_bm4w, bm4w, waits=wdm)
            c_dm = dma(RG, sAt, d_bmt, bmt_sb[:], waits=wdm)
            c_dm = dma(RG, sAt, d_bmtw, bmTw, waits=wdm)
            c_dm = dma(RG, sAt, d_csw, csw[:, 0:2048], waits=wdm)
            c_dm = dma(RG, sAt, d_mex, mexp1[:].rearrange("p a c -> p (a c)"),
                       waits=[(pV, cnt["V"])])
            c_dm = dma(RG, sAt, d_xnt, xnt[:].rearrange("p a b c -> p (a b c)"),
                       waits=[(sB0, ln1_b0), (sB1, ln1_b1)], max_dma_last_dim=4096)
            dbg_pre_attn = c_dm

        t_slab = []
        v_evac = []
        c_wcast = []

        def emit_attn_apply(s):
            b = s % 2
            w = [((sW0 if b == 0 else sW1), c_wcast[s])]
            if s < 2:
                w.append((pV, v_mask_done))
            else:
                w.append((pT, t_slab[s - 2]))
            return vop(lambda v, b=b, s=s: v.tensor_tensor(
                wts[:, b].rearrange("p a (j r) -> p a j r", r=32),
                wts[:, b].rearrange("p a (j r) -> p a j r", r=32),
                mexp1[:, :, nbs * s:nbs * (s + 1)].unsqueeze(3)
                    .broadcast_to([128, NKT, nbs, 32]),
                op=ALU.mult), waits=w)

        def emit_attn_evac(s):
            b = s % 2
            last = None
            for tt in range(n_tt):
                bank = pp[4 * b + tt // 2]
                c0 = 256 * (tt % 2)
                last = vop(lambda v, tt=tt, s=s, bank=bank, c0=c0: v.tensor_tensor(
                    xres[:, tt, slab_n * s:slab_n * (s + 1)],
                    bank[:, c0:c0 + slab_n],
                    xres[:, tt, slab_n * s:slab_n * (s + 1)], op=ALU.add),
                    waits=([(pT, t_slab[s])] if tt == 0 else ()),
                    selfw=(tt == 0), tick=(tt == n_tt - 1))
            v_evac.append(last)

        v_applies = []
        for s in range(n_slab):
            b = s % 2
            if s < 2:
                w = [(pV, v_red[n_mloc - 2 + s]), (sXc, c_xcast[n_tt - 1])]
                if debug:
                    w.append((sAt, dbg_pre_attn))
            else:
                w = [(pT, t_slab[s - 2])]
            c_wcast.append(dma(RG, (sW0 if b == 0 else sW1),
                               wts[:, b, :, :].rearrange("p a c -> p (a c)"),
                               wta[s], waits=w, max_dma_last_dim=8192))
            v_applies.append(emit_attn_apply(s))
            if s >= 1:
                pass
            w = [(pV, v_applies[s])]
            if s == 0:
                w.append((sB0, ln1_b0))
                w.append((sB1, ln1_b1))
            if s >= 2:
                w.append((pV, v_evac[s - 2]))
            first = True
            for kt in range(NKT):
                for tt in range(n_tt):
                    bank = pp[4 * b + tt // 2]
                    c0 = 256 * (tt % 2)
                    st = (kt == 0 and tt % 2 == 0)
                    mark = (kt == NKT - 1 and tt == n_tt - 1)
                    f = (lambda t, kt=kt, tt=tt, b=b, bank=bank, c0=c0, st=st: t.matmul(
                        bank[:, c0:c0 + slab_n],
                        xnt[:, tt, kt, :], wts[:, b, kt, :],
                        start=st, stop=(kt == NKT - 1),
                        skip_group_check=True))
                    if mark:
                        t_slab.append(tmark(f, waits=(w if first else ())))
                    else:
                        top(f, waits=(w if first else ()))
                    first = False
            if debug and s == 0:
                for tt in range(n_tt):
                    bank = pp[4 * b + tt // 2]
                    c0 = 256 * (tt % 2)
                    vop(lambda v, tt=tt, bank=bank, c0=c0: v.tensor_copy(
                        ost[:, 4 + tt, :], bank[:, c0:c0 + slab_n]),
                        waits=[(pT, t_slab[0])])
                v_datt = cnt["V"]
                c_datt = dma(RG, sAt, d_att[:, 0:n_tt * slab_n],
                             ost[:, 4:4 + n_tt, :], waits=[(pV, v_datt)])
            if s >= 1:
                emit_attn_evac(s - 1)
                if (s - 1) % 2 == 1 and slab_n == 256:
                    jj = (s - 1) // 2
                    for tt in range(n_tt):
                        vop(lambda v, tt=tt, jj=jj: v.bn_stats(
                            bnall[:, 48 * tt + 6 * jj:48 * tt + 6 * (jj + 1)],
                            xres[:, tt, 512 * jj:512 * (jj + 1)]),
                            selfw=(tt == 0), tick=(tt == n_tt - 1))
        emit_attn_evac(n_slab - 1)
        if slab_n == 256:
            jj = (n_slab - 1) // 2
            for tt in range(n_tt):
                vop(lambda v, tt=tt, jj=jj: v.bn_stats(
                    bnall[:, 48 * tt + 6 * jj:48 * tt + 6 * (jj + 1)],
                    xres[:, tt, 512 * jj:512 * (jj + 1)]),
                    selfw=(tt == 0), tick=(tt == n_tt - 1))
        v_attn_done = cnt["V"]
        t_attn_done = t_slab[-1]

        if debug:
            import os as _os
            _gate = int(_os.environ.get("X2GATE", str(n_slab - 1)))
            c_x2e = dma(RG, sAt, d_x2e, xres[:].rearrange("p a c -> p (a c)"),
                        waits=[(pV, v_evac[_gate]), (pT, t_attn_done) if _gate == n_slab - 1 else (pT, t_slab[_gate])],
                        max_dma_last_dim=4096)
            d_xp = nc.dram_tensor("d_xp", [128, n_tt * D], F32, kind="ExternalOutput").ap()
            _w = _os.environ.get("XPW", "ln1")
            if _w == "ln1":
                _wl = [(pV, ln1_ap[-1])]
            elif _w == "xbar":
                _wl = [(sB0, ln1_b0), (sB1, ln1_b1)]
            elif _w == "mask":
                _wl = [(pV, v_mask_done)]
            else:
                _wl = [(pV, v_apply)]
            if _w != "apply":
                dma(RG, sAt, d_xp, xres[:].rearrange("p a c -> p (a c)"),
                    waits=_wl, max_dma_last_dim=4096)
        ln2_ap = [None] * n_tt
        ln2_xb = [None] * n_tt
        ln2_aggr = [None] * n_tt

        c_at = dma(RG, sAt, at3.rearrange("p a c -> p (a c)"), at_l[:],
                   waits=[(pV, v_applies[-1])], max_dma_last_dim=8192)
        c_bp = dma(RG, sAt, bprime, bp_l[:], max_dma_last_dim=8192)
        c_bias = dma(RG, sAt, bias_sb[:, :], bias_in)

        tw = min(4, n_tt)
        nth = n_tt // tw
        nh = tw * 128
        t_rt = None
        t_ht = None

        def emit_router(tt):
            nonlocal t_rt
            sbp = sB0 if tt % 2 == 0 else sB1
            for kc in range(NKT):
                mark = (tt == n_tt - 1 and kc == NKT - 1)
                w = [(sbp, ln2_xb[tt])] if kc == 0 else ()
                if tt == 0 and kc == 0:
                    w = [(sbp, ln2_xb[tt]), (sWr, c_wrt), (pV, v_attn_done)]
                f = (lambda t, tt=tt, kc=kc: t.matmul(
                    pp[0][:, 8 * tt:8 * (tt + 1)], xnt[:, tt, kc, :],
                    wrt[:, 8 * kc:8 * (kc + 1)],
                    start=(tt == 0 and kc == 0), stop=(kc == NKT - 1),
                    skip_group_check=True))
                if mark:
                    t_rt = tmark(f, waits=w)
                else:
                    top(f, waits=w)

        def emit_ht(th):
            nonlocal t_ht
            for kc in range(NKT):
                mark = (th == nth - 1 and kc == NKT - 1)
                w = [(sAt, c_bias)] if (th == 0 and kc == 0) else ()
                f = (lambda t, th=th, kc=kc: t.matmul(
                    pp[1 + th][:, 0:nh], at3[:, kc, :],
                    xnt[:, tw * th:tw * (th + 1), kc, :], start=(kc == 0),
                    stop=(kc == NKT - 1), skip_group_check=True))
                if mark:
                    t_ht = tmark(f, waits=w)
                else:
                    top(f, waits=w)

        for tt in range(n_tt):
            wag = []
            if tt == 0:
                wag = [(pV, v_attn_done)]
                if debug:
                    wag.append((sAt, c_x2e))
            if slab_n == 256:
                ln2_aggr[tt] = vop(lambda v, tt=tt: v.bn_aggr(
                    mvall[:, 2 * tt:2 * (tt + 1)], bnall[:, 48 * tt:48 * (tt + 1)]),
                    waits=wag)
            else:
                pb = tt % 2
                for j in range(D // 512):
                    vop(lambda v, tt=tt, j=j, pb=pb: v.bn_stats(
                        bn_buf[pb][:, 6 * j:6 * (j + 1)],
                        xres[:, tt, 512 * j:512 * (j + 1)]),
                        waits=(wag if j == 0 else ()), selfw=(j == 0),
                        tick=(j == D // 512 - 1))
                ln2_aggr[tt] = vop(lambda v, tt=tt, pb=pb: v.bn_aggr(
                    mvall[:, 2 * tt:2 * (tt + 1)], bn_buf[pb][:]))
            pb = tt % 2
            a_sd = aop(lambda a, tt=tt, pb=pb: a.activation(
                sd_buf[pb][:], mvall[:, 2 * tt + 1:2 * tt + 2], AF.Sqrt, bias=epsv[:]),
                waits=[(pV, ln2_aggr[tt])])
            vop(lambda v, pb=pb: v.reciprocal(ri_buf[pb][:], sd_buf[pb][:]),
                waits=[(pA, a_sd)])
            vop(lambda v, tt=tt, pb=pb: v.tensor_scalar(
                nm_buf[pb][:], mvall[:, 2 * tt:2 * tt + 1], ri_buf[pb][:], -1.0,
                op0=ALU.mult, op1=ALU.mult))
            wap = []
            if tt >= 2:
                wap.append(((sB0 if tt % 2 == 0 else sB1), ln2_xb[tt - 2]))
            ln2_ap[tt] = vop(lambda v, tt=tt, pb=pb: v.tensor_scalar(
                xstg_ln2[pb], xres[:, tt, :], ri_buf[pb][:], nm_buf[pb][:],
                op0=ALU.mult, op1=ALU.add), waits=wap)
            w = [(pV, ln2_ap[tt]), (pT, t_attn_done)]
            ln2_xb[tt] = xbar((sB0 if tt % 2 == 0 else sB1),
                              xnt[:, tt, :, :], xstg_ln2[tt % 2], waits=w)
            emit_router(tt)
            if (tt + 1) % tw == 0:
                emit_ht((tt + 1) // tw - 1)
        ln2_b0, ln2_b1 = dcnt.get(sB0, 0), dcnt.get(sB1, 0)

        v_lg = vop(lambda v: v.tensor_copy(logits[:, 0:8 * n_tt], pp[0][:, 0:8 * n_tt]),
                   waits=[(pT, t_rt)])
        v_cmb = []
        for tt in range(n_tt):
            lg = logits[:, 8 * tt:8 * (tt + 1)]
            vop(lambda v, lg=lg: v.reduce_max(m1_t[:], lg, axis=AX.X))
            vop(lambda v, lg=lg: v.tensor_scalar(eq1_t[:], lg, m1_t[:], None,
                                                 op0=ALU.is_equal))
            vop(lambda v, lg=lg: v.scalar_tensor_tensor(l2_t[:], eq1_t[:], -1e30, lg,
                                                        op0=ALU.mult, op1=ALU.add))
            vop(lambda v: v.reduce_max(m2_t[:], l2_t[:], axis=AX.X))
            vop(lambda v: v.tensor_scalar(eq2_t[:], l2_t[:], m2_t[:], None,
                                          op0=ALU.is_equal))
            vop(lambda v: v.tensor_tensor(eqs_t[:], eq1_t[:], eq2_t[:], op=ALU.add))
            v_nm = vop(lambda v: v.tensor_scalar(nm1_t[:], m1_t[:], -1.0, None,
                                                 op0=ALU.mult))
            a_e = aop(lambda a, lg=lg: a.activation(ee_t[:], lg, AF.Exp, bias=nm1_t[:]),
                      waits=[(pV, v_nm)])
            vop(lambda v: v.tensor_tensor(cc_t[:], ee_t[:], eqs_t[:], op=ALU.mult),
                waits=[(pA, a_e)])
            vop(lambda v: v.reduce_sum(ssum_t[:], cc_t[:], axis=AX.X))
            vop(lambda v: v.reciprocal(rs_t[:], ssum_t[:]))
            v_cmb.append(vop(lambda v, tt=tt: v.tensor_scalar(
                combine[:, 8 * tt:8 * (tt + 1)], cc_t[:], rs_t[:], None, op0=ALU.mult)))
        ct_bank = [pp[3], pp[0]]
        t_ct = []
        for tt in range(n_tt):
            bank = ct_bank[tt // tw]
            w = [(pV, v_cmb[tt])]
            if tt == tw:
                w.append((pV, v_lg))
            t_ct.append(tmark(lambda t, tt=tt, bank=bank: t.matmul(
                bank[0:8, 128 * (tt % tw):128 * (tt % tw + 1)],
                combine[:, 8 * tt:8 * (tt + 1)], id_b[:],
                start=(tt % tw == 0), stop=(tt % tw == tw - 1 or tt == n_tt - 1),
                skip_group_check=True), waits=w))
        v_ct = []
        for h in range(nth):
            v_ct.append(vop(lambda v, h=h: v.tensor_copy(
                combT[0:8, nh * h:nh * (h + 1)], ct_bank[h][0:8, 0:nh]),
                waits=[(pT, t_ct[min(tw * h + tw - 1, n_tt - 1)])]))
        t_cte, v_cte = [], []
        for th in range(nth):
            w = [(pV, v_ct[th])]
            if th >= 1:
                w.append((pV, v_cte[th - 1]))
            t_cte.append(tmark(lambda t, th=th: t.matmul(
                pp[3][:, 0:nh], s16_sb[:], combT[0:8, nh * th:nh * (th + 1)],
                start=True, stop=True, skip_group_check=True),
                waits=w))
            v_cte.append(vop(lambda v, th=th: v.tensor_copy(
                hwt[:, nh * th:nh * (th + 1)], pp[3][:, 0:nh]),
                waits=[(pT, t_cte[th])]))
        v_hwt = []
        for th in range(nth):
            v_hwt.append(vop(lambda v, th=th: v.tensor_tensor(
                hwt[:, nh * th:nh * (th + 1)], pp[1 + th][:, 0:nh],
                hwt[:, nh * th:nh * (th + 1)], op=ALU.mult),
                waits=[(pT, t_ht), (pV, v_cte[th])]))
        v_prep_done = v_hwt[-1]

        t_fslab, v_fevac, c_store = [], [], []
        y2 = y_out.rearrange("(a p) k -> p a k", p=128)
        for s in range(n_slab):
            b = s % 2
            w = [(sB0, ln2_b0), (sB1, ln2_b1), (sAt, c_bias)]
            if s >= 2:
                w.append((pT, t_fslab[s - 2]))
            else:
                w.append((pT, t_attn_done))
            c_w = dma(RG, (sW0 if b == 0 else sW1),
                      wts[:, b, :, :].rearrange("p a c -> p (a c)"),
                      wtb[s], waits=w, max_dma_last_dim=8192)
            c_wcast.append(c_w)
            bps = (s + 1) % 2
            w = [((sW0 if b == 0 else sW1), c_w)]
            if s >= 2:
                w.append((pV, v_fevac[s - 2]))
            elif s == 0:
                w.append((pV, v_attn_done))
            else:
                w.append((pV, v_prep_done))
            first = True
            for kt in range(NKT):
                for tt in range(n_tt):
                    bank = pp[4 * bps + tt // 2]
                    top(lambda t, kt=kt, tt=tt, b=b, bank=bank: t.matmul(
                        bank[:, 256 * (tt % 2):256 * (tt % 2) + slab_n],
                        xnt[:, tt, kt, :], wts[:, b, kt, :],
                        start=(kt == 0 and tt % 2 == 0), stop=False,
                        skip_group_check=True), waits=(w if first else ()))
                    first = False
            for tt in range(n_tt):
                bank = pp[4 * bps + tt // 2]
                wl = [(pV, v_prep_done)] if (s == 0 and tt == 0) else ()
                top(lambda t, tt=tt, s=s, bank=bank: t.matmul(
                    bank[:, 256 * (tt % 2):256 * (tt % 2) + slab_n],
                    hwt[:, 128 * tt:128 * (tt + 1)],
                    bprime[:, slab_n * s:slab_n * (s + 1)],
                    start=False, stop=False, skip_group_check=True), waits=wl)
                mark = (tt == n_tt - 1)
                f = (lambda t, tt=tt, s=s, bank=bank: t.matmul(
                    bank[:, 256 * (tt % 2):256 * (tt % 2) + slab_n],
                    ones_sb[:], bias_sb[:, slab_n * s:slab_n * (s + 1)],
                    start=False, stop=True, skip_group_check=True))
                if mark:
                    t_fslab.append(tmark(f))
                else:
                    top(f)
            last = None
            for tt in range(n_tt):
                bank = pp[4 * bps + tt // 2]
                w2 = []
                if tt == 0:
                    w2 = [(pT, t_fslab[s])]
                    if s >= 1:
                        w2.append((sSt, c_store[s - 1]))
                last = vop(lambda v, tt=tt, s=s, bank=bank: v.tensor_tensor(
                    ost[:, tt, :], bank[:, 256 * (tt % 2):256 * (tt % 2) + slab_n],
                    xres[:, tt, slab_n * s:slab_n * (s + 1)], op=ALU.add),
                    waits=w2, selfw=(tt == 0), tick=(tt == n_tt - 1))
            v_fevac.append(last)
            c_store.append(dma(RY, sSt,
                               y2[:, :, slab_n * s:slab_n * (s + 1)], ost[:, 0:n_tt, :],
                               waits=[(pV, v_fevac[s])]))
        c_final = dcnt[sSt]
        if debug:
            wfin = [(sSt, c_final), (pV, cnt["V"]), (pA, cnt["A"]), (pT, cnt["T"])]
            dma(RG, sAt, d_xres, xres[:].rearrange("p a c -> p (a c)"),
                waits=wfin, max_dma_last_dim=4096)
            dma(RG, sAt, d_comb, combine[:], waits=wfin)
            dma(RG, sAt, d_hwt, hwt[:], waits=wfin)
            c_final2 = dcnt[sAt]

        with nc.Block() as blk:
            @blk.gpsimd
            def _(g):
                for f in RG:
                    f(g)

            @blk.sync
            def _(s):
                for f in RY:
                    f(s)
                s.wait_ge(sSt, c_final)
                if debug:
                    s.wait_ge(sAt, c_final2)

            @blk.vector
            def _(v):
                for f in RV:
                    f(v)

            @blk.scalar
            def _(a):
                for f in RA:
                    f(a)

            @blk.tensor
            def _(t):
                for f in RT:
                    f(t)

    return nc


def _lay_weights(attn_w, base_w, a_w, b_w, wr_w, n_slab=16, slab_n=256):
    wt = np.ascontiguousarray(
        attn_w.T.reshape(NKT, 128, n_slab, slab_n).transpose(2, 1, 0, 3)
    ).reshape(n_slab, 128, NKT * slab_n)
    wtb = np.ascontiguousarray(
        base_w.T.reshape(NKT, 128, n_slab, slab_n).transpose(2, 1, 0, 3)
    ).reshape(n_slab, 128, NKT * slab_n)
    at = np.ascontiguousarray(
        a_w.reshape(128, NKT, 128).transpose(2, 1, 0)).reshape(128, D)
    bp = np.ascontiguousarray(b_w.transpose(0, 2, 1)).reshape(128, D)
    wrt = np.ascontiguousarray(
        wr_w.reshape(8, NKT, 128).transpose(2, 1, 0)).reshape(128, NKT * 8)
    return wt, wtb, at, bp, wrt


def _consts():
    p = np.arange(128)
    identb = np.eye(128, dtype=np.float32).astype(ml_dtypes.bfloat16)
    identf = None
    ones1 = np.ones((1, 128), dtype=ml_dtypes.bfloat16)
    bdiag4 = (p[:, None] // 32 == np.arange(4)[None, :]).astype(np.float32).astype(ml_dtypes.bfloat16)
    s4m = (np.arange(4)[:, None] == p[None, :] // 32).astype(np.float32).astype(ml_dtypes.bfloat16)
    s16 = (2.0 * (np.arange(8)[:, None] == p[None, :] // 16)).astype(np.float32).astype(ml_dtypes.bfloat16)
    epsv = np.full((128, 1), 1e-5, dtype=np.float32)
    return identb, identf, ones1, bdiag4, s4m, s16, epsv


def kernel(L0_attn_data, L0_attn_weight, L0_attn_mask, L0_ffn_W_base,
           L0_ffn_b_base, L0_ffn_W_router, L0_ffn_A, L0_ffn_B):
    x2d = np.ascontiguousarray(np.asarray(L0_attn_data, dtype=np.float32).reshape(8192, D))
    wt, wtb, at, bp, wrt = _lay_weights(
        np.asarray(L0_attn_weight, dtype=np.float32),
        np.asarray(L0_ffn_W_base, dtype=np.float32),
        np.asarray(L0_ffn_A, dtype=np.float32).reshape(128, D),
        np.asarray(L0_ffn_B, dtype=np.float32),
        np.asarray(L0_ffn_W_router, dtype=np.float32))
    mask = np.asarray(L0_attn_mask, dtype=np.int32)
    bias = np.asarray(L0_ffn_b_base, dtype=np.float32).reshape(1, D)
    identb, identf, ones1, bdiag4, s4m, s16, epsv = _consts()

    nc = build_core_program()
    shared = dict(wta=wt, wtb=wtb, at_l=at, bp_l=bp, wrt_l=wrt,
                  bias_in=bias, epsv_in=epsv, identb=identb,
                  ones1=ones1, bdiag4=bdiag4, s4m=s4m, s16x2=s16)
    in_maps = []
    for c in range(N_CORES):
        m = dict(shared)
        m["x_in"] = np.ascontiguousarray(x2d[1024 * c:1024 * (c + 1), :])
        m["mask_in"] = np.ascontiguousarray(mask[512 * c:512 * (c + 1), :])
        in_maps.append(m)
    res = run_bass_kernel_spmd(nc, in_maps, core_ids=list(range(N_CORES)))
    out = np.concatenate([r["y_out"] for r in res.results], axis=0)
    return out.reshape(2, 4096, 4096)


# version 23
# speedup vs baseline: 1.1023x; 1.0471x over previous
import numpy as np
import ml_dtypes
import concourse.bass as bass
import concourse.mybir as mybir
from concourse.bass_utils import run_bass_kernel_spmd

F32 = mybir.dt.float32
BF16 = mybir.dt.bfloat16
I32 = mybir.dt.int32
AF = mybir.ActivationFunctionType
ALU = mybir.AluOpType
AX = mybir.AxisListType

D = 4096
NKT = 32
NB = 128
N_CORES = 8


def build_core_program(n_tt=8, n_slab=16, slab_n=256, debug=False, sep_banks=False, skipmask=False):
    T = n_tt * 128
    assert n_slab * slab_n == D
    nbs = slab_n // 32

    import os as _os0
    _scr = int(_os0.environ.get("DMASCRATCH", "16384"))
    nc = bass.Bass("TRN2", target_bir_lowering=False, dynamic_dma_scratch_size=_scr)
    x_in = nc.dram_tensor("x_in", [T, D], F32, kind="ExternalInput").ap()
    wta = nc.dram_tensor("wta", [n_slab, 128, NKT * slab_n], F32, kind="ExternalInput").ap()
    wtb = nc.dram_tensor("wtb", [n_slab, 128, NKT * slab_n], F32, kind="ExternalInput").ap()
    mask_in = nc.dram_tensor("mask_in", [D // 8, D], I32, kind="ExternalInput").ap()
    bmg_in = nc.dram_tensor("bmg_in", [4, D // 8], BF16, kind="Internal").ap()
    bmg_out = nc.dram_tensor("bmg_out", [32, D // 8], BF16, kind="Internal").ap()
    bmscr = nc.dram_tensor("bmscr", [128, 128], BF16, kind="Internal").ap()
    at_l = nc.dram_tensor("at_l", [128, D], F32, kind="ExternalInput").ap()
    bp_l = nc.dram_tensor("bp_l", [128, D], F32, kind="ExternalInput").ap()
    wrt_l = nc.dram_tensor("wrt_l", [128, 256], F32, kind="ExternalInput").ap()
    bias_in = nc.dram_tensor("bias_in", [1, D], F32, kind="ExternalInput").ap()
    epsv_in = nc.dram_tensor("epsv_in", [128, 1], F32, kind="ExternalInput").ap()
    identb = nc.dram_tensor("identb", [128, 128], BF16, kind="ExternalInput").ap()
    ones1 = nc.dram_tensor("ones1", [1, 128], BF16, kind="ExternalInput").ap()
    bdiag4 = nc.dram_tensor("bdiag4", [128, 4], BF16, kind="ExternalInput").ap()
    s4m = nc.dram_tensor("s4m", [4, 128], BF16, kind="ExternalInput").ap()
    s16x2 = nc.dram_tensor("s16x2", [8, 128], BF16, kind="ExternalInput").ap()
    y_out = nc.dram_tensor("y_out", [T, D], F32, kind="ExternalOutput").ap()
    if debug:
        d_xres = nc.dram_tensor("d_xres", [128, n_tt * D], F32, kind="ExternalOutput").ap()
        d_stats = nc.dram_tensor("d_stats", [128, 2 * n_tt], F32, kind="ExternalOutput").ap()
        d_comb = nc.dram_tensor("d_comb", [128, 8 * n_tt], F32, kind="ExternalOutput").ap()
        d_mex = nc.dram_tensor("d_mex", [128, D], F32, kind="ExternalOutput").ap()
        d_hwt = nc.dram_tensor("d_hwt", [128, T], F32, kind="ExternalOutput").ap()
        d_xnt = nc.dram_tensor("d_xnt", [128, n_tt * NKT * 128], F32, kind="ExternalOutput").ap()
        d_csw = nc.dram_tensor("d_csw", [128, 2048], F32, kind="ExternalOutput").ap()
        d_bm4w = nc.dram_tensor("d_bm4w", [4, D], F32, kind="ExternalOutput").ap()
        d_bmt = nc.dram_tensor("d_bmt", [128, NB], F32, kind="ExternalOutput").ap()
        d_bmtw = nc.dram_tensor("d_bmtw", [4, D], F32, kind="ExternalOutput").ap()
        d_att = nc.dram_tensor("d_att", [128, 1024], F32, kind="ExternalOutput").ap()
        d_x2e = nc.dram_tensor("d_x2e", [128, n_tt * D], F32, kind="ExternalOutput").ap()

    from contextlib import ExitStack
    with ExitStack() as ctx:
        e = ctx.enter_context
        xres = e(nc.sbuf_tensor([128, n_tt, D], BF16))
        xnt = e(nc.sbuf_tensor([128, n_tt, NKT, 128], BF16))
        wts = e(nc.sbuf_tensor([128, 2, NKT, slab_n], BF16))
        r2 = e(nc.sbuf_tensor([128, 2 * D], BF16))
        xstg = e(nc.sbuf_tensor([128, D], BF16))
        ost = e(nc.sbuf_tensor([128, max(n_tt, 8), slab_n], F32))
        hwt = e(nc.sbuf_tensor([128, T], BF16))
        wrt = e(nc.sbuf_tensor([128, 256], BF16))

        logits = e(nc.sbuf_tensor([128, 8 * n_tt], F32))
        combine = e(nc.sbuf_tensor([128, 8 * n_tt], BF16))
        packed = e(nc.sbuf_tensor([128, D], BF16))
        bmt_sb = e(nc.sbuf_tensor([128, NB], BF16))
        s1s = e(nc.sbuf_tensor([128, n_tt], F32))
        s2s = e(nc.sbuf_tensor([128, n_tt], F32))
        scr = e(nc.sbuf_tensor([128, 192], F32))
        bnall = e(nc.sbuf_tensor([128, 8 * 6 * 8], F32))
        mvall = e(nc.sbuf_tensor([128, 16], F32))
        mean_t = scr[:, 0:1]; ex2_t = scr[:, 1:2]; msq_t = scr[:, 2:3]
        var_t = scr[:, 3:4]; sd_t = scr[:, 4:5]; rinv_t = scr[:, 5:6]
        negmr_t = scr[:, 6:7]; epsv = scr[:, 7:8]; m1_t = scr[:, 8:9]
        m2_t = scr[:, 9:10]; nm1_t = scr[:, 10:11]; ssum_t = scr[:, 11:12]
        rs_t = scr[:, 12:13]
        eq1_t = scr[:, 16:24]; eq2_t = scr[:, 24:32]; l2_t = scr[:, 32:40]
        eqs_t = scr[:, 40:48]; ee_t = scr[:, 48:56]; cc_t = scr[:, 56:64]
        bn_buf = [scr[:, 64:112], scr[:, 116:164]]
        mv_buf = [scr[:, 112:114], scr[:, 114:116]]
        sd_buf = [scr[:, 164:165], scr[:, 165:166]]
        ri_buf = [scr[:, 166:167], scr[:, 167:168]]
        nm_buf = [scr[:, 168:169], scr[:, 169:170]]
        id_b = e(nc.sbuf_tensor([128, 128], BF16))
        ones_sb = e(nc.sbuf_tensor([1, 128], BF16))
        bd4_sb = e(nc.sbuf_tensor([128, 4], BF16))
        s4_sb = e(nc.sbuf_tensor([4, 128], BF16))
        s16_sb = e(nc.sbuf_tensor([8, 128], BF16))
        pp = [e(nc.psum_tensor(f"pp{i}", [128, 512], F32)) for i in range(8)]
        sSm = e(nc.semaphore("sSm"))
        sXc = e(nc.semaphore("sXc"))
        sW0 = e(nc.semaphore("sW0"))
        sW1 = e(nc.semaphore("sW1"))
        sM0 = e(nc.semaphore("sM0"))
        sM1 = e(nc.semaphore("sM1"))
        sB0 = e(nc.semaphore("sB0"))
        sB1 = e(nc.semaphore("sB1"))
        sAsm = e(nc.semaphore("sAsm"))
        sAsm2 = e(nc.semaphore("sAsm2"))
        sSt = e(nc.semaphore("sSt"))
        sAt = e(nc.semaphore("sAt"))
        sWr = e(nc.semaphore("sWr"))
        sCc = e(nc.semaphore("sCc"))
        sCol = e(nc.semaphore("sCol"))
        pV = e(nc.semaphore("pV"))
        pA = e(nc.semaphore("pA"))
        pT = e(nc.semaphore("pT"))

        mview = [wts[:, b, :, :].rearrange("p a c -> p (a c)").bitcast(I32)
                 for b in range(2)]
        mexp1 = r2[:, 0:D].rearrange("p (a c) -> p a c", a=NKT)
        at3 = r2[:, 0:D].rearrange("p (a c) -> p a c", a=NKT)
        bprime = r2[:, D:2 * D]
        csw = ost[:].rearrange("p a c -> p (a c)").bitcast(BF16)
        junk = [xstg[:], xstg[:]]
        xnt7 = xnt[:, n_tt - 1].rearrange("p a b -> p (a b)") if n_tt >= 8 else None
        def ln1_src(tt):
            if n_tt < 8 or tt % 2 == 0 or tt == 7:
                return xstg[:]
            return xnt7
        xstg_ln2 = [xstg[:], ost[:].rearrange("p a c -> p (a c)").bitcast(BF16)[:, 0:D]]
        ln2_stage = wts[:, 0].rearrange("p a c -> p (a c)")
        bm4w = packed[0:4, 0:D]
        bmTw = packed[0:4, 0:D]
        bias_sb = packed[0:1, 0:D]
        combT = ost[:].rearrange("p a c -> p (a c)").bitcast(BF16)[0:8, 0:T]

        RG, RY, RV, RA, RT = [], [], [], [], []
        cnt = {"V": 0, "A": 0, "T": 0}
        dcnt = {}

        def dma(lst, sem, out_ap, in_ap, waits=(), **kw):
            def f(eng, out_ap=out_ap, in_ap=in_ap, kw=kw, waits=tuple(waits), sem=sem):
                for (s_, v_) in waits:
                    eng.wait_ge(s_, v_)
                eng.dma_start(out_ap, in_ap, **kw).then_inc(sem, 16)
            lst.append(f)
            dcnt[sem] = dcnt.get(sem, 0) + 16
            return dcnt[sem]

        def xbar(sem, out_ap, in_ap, waits=()):
            def f(eng, out_ap=out_ap, in_ap=in_ap, waits=tuple(waits), sem=sem):
                for (s_, v_) in waits:
                    eng.wait_ge(s_, v_)
                eng.dma_start(out_ap, in_ap, transpose=True).then_inc(sem, 16)
            RY.append(f)
            dcnt[sem] = dcnt.get(sem, 0) + 16
            return dcnt[sem]

        def op(lst, key, sem, builder, waits=(), selfw=True, tick=True):
            myc = cnt[key]
            do_tick = tick
            def f(eng, builder=builder, waits=tuple(waits), myc=myc, sem=sem,
                  selfw=selfw, do_tick=do_tick):
                if selfw and myc:
                    eng.wait_ge(sem, myc)
                for (s_, v_) in waits:
                    eng.wait_ge(s_, v_)
                ins = builder(eng)
                if do_tick:
                    ins.then_inc(sem, 1)
            lst.append(f)
            if do_tick:
                cnt[key] += 1
            return cnt[key]

        def vop(builder, waits=(), selfw=True, tick=True):
            return op(RV, "V", pV, builder, waits, selfw, tick)

        def aop(builder, waits=(), selfw=True, tick=True):
            return op(RA, "A", pA, builder, waits, selfw, tick)

        def top(builder, waits=(), selfw=False, tick=False):
            return op(RT, "T", pT, builder, waits, selfw, tick)

        def tmark(builder, waits=()):
            return op(RT, "T", pT, builder, waits, selfw=True, tick=True)

        c_small = 0
        for (dst, srcap) in ((epsv[:], epsv_in), (id_b[:], identb),
                           (ones_sb[:], ones1), (bd4_sb[:], bdiag4), (s4_sb[:], s4m),
                           (s16_sb[:], s16x2)):
            c_small = dma(RY, sSm, dst, srcap)
        c_wrt = dma(RG, sWr, wrt[:], wrt_l)
        xr = x_in.rearrange("(a p) k -> p a k", p=128)
        c_xcast = []
        rg_cc_idx = None
        for tt in range(n_tt):
            w = [(sXc, 16 * tt)] if tt >= 1 else ()
            c_xcast.append(dma(RG, sXc, xres[:, tt, :], xr[:, tt, :],
                               waits=w, max_dma_last_dim=8192))
            if tt == min(2, n_tt - 1):
                rg_cc_idx = len(RG)
        c_xcast_all = dcnt[sXc]

        def emit_ln_sqrt(tt, v_aggr, extra=()):
            pb = tt % 2
            return aop(lambda a, pb=pb: a.activation(
                sd_buf[pb][:], mv_buf[pb][:, 1:2], AF.Sqrt, bias=epsv[:]),
                waits=[(pV, v_aggr)] + list(extra))

        m2d = mask_in
        c_mload, v_red, t_bd, v_gt, t_piece = [], [], [], [], []
        ln1_xb = [None] * n_tt
        ln1_ap = [None] * n_tt
        ln1_aggr = [None] * n_tt
        ln1_sqrt = [None] * n_tt
        n_mloc = D // 8 // 128
        for i in range(n_mloc):
            qs = (i % 2 == 0)
            lst = RY if qs else RA
            sem = sM0 if qs else sM1
            w = []
            if i >= 2:
                w.append((pV, v_red[i - 2]))
            c_mload.append(dma(lst, sem, mview[i % 2],
                               m2d[128 * i:128 * (i + 1), :], waits=w))
            def _red(v, i=i):
                with nc.allow_low_precision(reason="mask block sums <=32, exact in bf16"):
                    return v.tensor_reduce(
                        csw[:, NB * i:NB * (i + 1)],
                        mview[i % 2].rearrange("p (a c) -> p a c", c=32),
                        axis=AX.X, op=ALU.add)
            v_red.append(vop(_red, waits=[(sem, c_mload[i])]))
            bank = pp[i % 2]
            w = [(pV, v_red[i])]
            if i == 0:
                w.append((sSm, c_small))
            t_bd.append(tmark(
                lambda t, i=i, bank=bank: t.matmul(
                    bank[0:4, 0:NB], bd4_sb[:], csw[:, NB * i:NB * (i + 1)],
                    start=True, stop=True, skip_group_check=True),
                waits=w))
            v_gt.append(vop(
                lambda v, i=i, bank=bank: v.tensor_single_scalar(
                    bm4w[:, NB * i:NB * (i + 1)], bank[0:4, 0:NB], 0.0, op=ALU.is_gt),
                waits=[(pT, t_bd[i])]))
        c_bmout = dma(RY, sCc, bmg_in, bm4w[:, 0:NB * n_mloc],
                      waits=[(pV, v_gt[-1])])
        def _cc(eng):
            eng.wait_ge(sCc, c_bmout)
            eng.collective_compute(
                "AllGather", ALU.bypass,
                replica_groups=[list(range(N_CORES))],
                ins=[bmg_in.opt()], outs=[bmg_out.opt()]).then_inc(sCol, 1)
        RG.insert(rg_cc_idx, _cc)

        junkA = r2[:, D:2 * D]
        v_nm = [None] * n_tt
        a_ln1 = [None] * n_tt

        def emit_ln1_stats(tt):
            pb = tt % 2
            if tt % 2 == 0:
                wbn = [(sXc, c_xcast[tt])]
                for j in range(D // 512):
                    vop(lambda v, tt=tt, j=j, pb=pb: v.bn_stats(
                        bn_buf[pb][:, 6 * j:6 * (j + 1)],
                        xres[:, tt, 512 * j:512 * (j + 1)]),
                        waits=(wbn if j == 0 else ()), selfw=(j == 0),
                        tick=(j == D // 512 - 1))
                ln1_aggr[tt] = vop(lambda v, pb=pb: v.bn_aggr(
                    mv_buf[pb][:], bn_buf[pb][:]))
                return
            w = [(sXc, c_xcast[tt])]
            if tt >= 2:
                w.append((pV, v_nm[tt - 2]))
            aop(lambda a, tt=tt, pb=pb: a.activation(
                junkA, xres[:, tt, :], AF.Copy, accum_out=mv_buf[pb][:, 0:1]),
                waits=w)
            a_ln1[tt] = aop(lambda a, tt=tt, pb=pb: a.activation(
                junkA, xres[:, tt, :], AF.Square, accum_out=mv_buf[pb][:, 1:2]),
                waits=[(pV, v_nm[tt - 2])] if tt >= 2 else ())
            vop(lambda v, pb=pb: v.tensor_single_scalar(
                mv_buf[pb][:, 0:1], mv_buf[pb][:, 0:1], 1.0 / D, op=ALU.mult),
                waits=[(pA, a_ln1[tt])])
            vop(lambda v, pb=pb: v.tensor_tensor(
                sd_buf[pb][:], mv_buf[pb][:, 0:1], mv_buf[pb][:, 0:1],
                op=ALU.mult))
            ln1_aggr[tt] = vop(lambda v, pb=pb: v.scalar_tensor_tensor(
                mv_buf[pb][:, 1:2], mv_buf[pb][:, 1:2], 1.0 / D, sd_buf[pb][:],
                op0=ALU.mult, op1=ALU.subtract))

        emit_ln1_stats(0)
        if n_tt > 1:
            emit_ln1_stats(1)
        for tt in range(n_tt):
            ln1_sqrt[tt] = emit_ln_sqrt(tt, ln1_aggr[tt],
                                        extra=([(sSm, c_small)] if tt == 0 else ()))
            pb = tt % 2
            vop(lambda v, pb=pb: v.reciprocal(ri_buf[pb][:], sd_buf[pb][:]),
                waits=[(pA, ln1_sqrt[tt])])
            v_nm[tt] = vop(lambda v, pb=pb: v.tensor_scalar(
                nm_buf[pb][:], mv_buf[pb][:, 0:1], ri_buf[pb][:], -1.0,
                op0=ALU.mult, op1=ALU.mult))
            wap = []
            if n_tt < 8:
                if tt >= 1:
                    wap.append(((sB0 if (tt - 1) % 2 == 0 else sB1), ln1_xb[tt - 1]))
            elif tt >= 2:
                wap.append(((sB0 if tt % 2 == 0 else sB1), ln1_xb[tt - 2]))
            if tt == 7 and n_tt >= 8:
                wap.append((sB0, ln1_xb[6]))
            ln1_ap[tt] = vop(lambda v, tt=tt, pb=pb: v.tensor_scalar(
                ln1_src(tt), xres[:, tt, :], ri_buf[pb][:], nm_buf[pb][:],
                op0=ALU.mult, op1=ALU.add), waits=wap)
            w = [(pV, ln1_ap[tt])]
            ln1_xb[tt] = xbar((sB0 if tt % 2 == 0 else sB1),
                              xnt[:, tt, :, :], ln1_src(tt), waits=w)
            if tt + 2 < n_tt:
                emit_ln1_stats(tt + 2)
        ln1_b0, ln1_b1 = dcnt.get(sB0, 0), dcnt.get(sB1, 0)

        c_bmin = dma(RY, sCc, bm4w[:].rearrange("j (r di kb) -> j r di kb",
                                                r=N_CORES, di=n_mloc),
                     bmg_out[:].rearrange("(r j) (di kb) -> j r di kb",
                                          j=4, kb=NB),
                     waits=[(sCol, 1)])
        c_bm_ready = c_bmin

        n_mtile = D // 128
        for i in range(n_mtile):
            t_piece.append(tmark(
                lambda t, i=i: t.matmul(
                    pp[2][:, 4 * i:4 * (i + 1)], bm4w[:, NB * i:NB * (i + 1)],
                    id_b[0:4, 0:4], start=(i == 0), stop=(i == n_mtile - 1),
                    skip_group_check=True),
                waits=([(sCc, c_bm_ready)] if i == 0 else ())))

        v_bmt = vop(lambda v: v.tensor_copy(bmt_sb[:], pp[2][:, 0:NB]),
                    waits=[(pT, t_piece[-1])])
        c_asm1 = dma(RY, sAsm, bmscr, bmt_sb[:], waits=[(pV, v_bmt)])
        c_asm = dma(RY, sAsm, bmTw[:].rearrange("k (kt j) -> k kt j", j=NB),
                    bmscr.rearrange("(kt k) j -> k kt j", k=4),
                    waits=[(sAsm, c_asm1)])
        t_mex, v_mex = [], []
        for c in range(NKT // 4):
            bank = pp[3 + (c % 2)]
            w = [(sAsm, c_asm)] if c == 0 else ()
            if c >= 2:
                w = list(w) + [(pV, v_mex[c - 2])]
            t_mex.append(tmark(
                lambda t, c=c, bank=bank: t.matmul(
                    bank[:], s4_sb[:], bmTw[:, 512 * c:512 * (c + 1)],
                    start=True, stop=True, skip_group_check=True),
                waits=w))
            v_mex.append(vop(
                lambda v, c=c, bank=bank: v.tensor_copy(
                    mexp1[:, 4 * c:4 * (c + 1), :], bank[:]),
                waits=[(pT, t_mex[c])]))
        v_mask_done = v_mex[-1]

        if debug:
            wdm = [(pV, cnt["V"]), (pT, cnt["T"]),
                   (sB0, ln1_b0), (sB1, ln1_b1)]
            c_dm = dma(RG, sAt, d_bm4w, bm4w, waits=wdm)
            c_dm = dma(RG, sAt, d_bmt, bmt_sb[:], waits=wdm)
            c_dm = dma(RG, sAt, d_bmtw, bmTw, waits=wdm)
            c_dm = dma(RG, sAt, d_csw, csw[:, 0:2048], waits=wdm)
            c_dm = dma(RG, sAt, d_mex, mexp1[:].rearrange("p a c -> p (a c)"),
                       waits=[(pV, cnt["V"])])
            c_dm = dma(RG, sAt, d_xnt, xnt[:].rearrange("p a b c -> p (a b c)"),
                       waits=[(sB0, ln1_b0), (sB1, ln1_b1)], max_dma_last_dim=4096)
            dbg_pre_attn = c_dm

        t_slab = []
        v_evac = []
        c_wcast = []

        def emit_attn_apply(s):
            b = s % 2
            w = [((sW0 if b == 0 else sW1), c_wcast[s])]
            if s < 2:
                w.append((pV, v_mask_done))
            else:
                w.append((pT, t_slab[s - 2]))
            return vop(lambda v, b=b, s=s: v.tensor_tensor(
                wts[:, b].rearrange("p a (j r) -> p a j r", r=32),
                wts[:, b].rearrange("p a (j r) -> p a j r", r=32),
                mexp1[:, :, nbs * s:nbs * (s + 1)].unsqueeze(3)
                    .broadcast_to([128, NKT, nbs, 32]),
                op=ALU.mult), waits=w)

        def emit_attn_evac(s):
            b = s % 2
            last = None
            for tt in range(n_tt):
                bank = pp[4 * b + tt // 2]
                c0 = 256 * (tt % 2)
                last = vop(lambda v, tt=tt, s=s, bank=bank, c0=c0: v.tensor_tensor(
                    xres[:, tt, slab_n * s:slab_n * (s + 1)],
                    bank[:, c0:c0 + slab_n],
                    xres[:, tt, slab_n * s:slab_n * (s + 1)], op=ALU.add),
                    waits=([(pT, t_slab[s])] if tt == 0 else ()),
                    selfw=(tt == 0), tick=(tt == n_tt - 1))
            v_evac.append(last)

        v_applies = []
        for s in range(n_slab):
            b = s % 2
            if s < 2:
                w = [(pV, v_red[n_mloc - 2 + s]), (sXc, c_xcast[n_tt - 1])]
                if debug:
                    w.append((sAt, dbg_pre_attn))
            else:
                w = [(pT, t_slab[s - 2])]
            c_wcast.append(dma(RG, (sW0 if b == 0 else sW1),
                               wts[:, b, :, :].rearrange("p a c -> p (a c)"),
                               wta[s], waits=w, max_dma_last_dim=8192))
            v_applies.append(emit_attn_apply(s))
            if s >= 1:
                pass
            w = [(pV, v_applies[s])]
            if s == 0:
                w.append((sB0, ln1_b0))
                w.append((sB1, ln1_b1))
            if s >= 2:
                w.append((pV, v_evac[s - 2]))
            first = True
            for kt in range(NKT):
                for tt in range(n_tt):
                    bank = pp[4 * b + tt // 2]
                    c0 = 256 * (tt % 2)
                    st = (kt == 0 and tt % 2 == 0)
                    mark = (kt == NKT - 1 and tt == n_tt - 1)
                    f = (lambda t, kt=kt, tt=tt, b=b, bank=bank, c0=c0, st=st: t.matmul(
                        bank[:, c0:c0 + slab_n],
                        xnt[:, tt, kt, :], wts[:, b, kt, :],
                        start=st, stop=(kt == NKT - 1),
                        skip_group_check=True))
                    if mark:
                        t_slab.append(tmark(f, waits=(w if first else ())))
                    else:
                        top(f, waits=(w if first else ()))
                    first = False
            if debug and s == 0:
                for tt in range(n_tt):
                    bank = pp[4 * b + tt // 2]
                    c0 = 256 * (tt % 2)
                    vop(lambda v, tt=tt, bank=bank, c0=c0: v.tensor_copy(
                        ost[:, 4 + tt, :], bank[:, c0:c0 + slab_n]),
                        waits=[(pT, t_slab[0])])
                v_datt = cnt["V"]
                c_datt = dma(RG, sAt, d_att[:, 0:n_tt * slab_n],
                             ost[:, 4:4 + n_tt, :], waits=[(pV, v_datt)])
            if s >= 1:
                emit_attn_evac(s - 1)
                if (s - 1) % 2 == 1 and slab_n == 256:
                    jj = (s - 1) // 2
                    for tt in range(n_tt):
                        vop(lambda v, tt=tt, jj=jj: v.bn_stats(
                            bnall[:, 48 * tt + 6 * jj:48 * tt + 6 * (jj + 1)],
                            xres[:, tt, 512 * jj:512 * (jj + 1)]),
                            selfw=(tt == 0), tick=(tt == n_tt - 1))
        emit_attn_evac(n_slab - 1)
        if slab_n == 256:
            jj = (n_slab - 1) // 2
            for tt in range(n_tt):
                vop(lambda v, tt=tt, jj=jj: v.bn_stats(
                    bnall[:, 48 * tt + 6 * jj:48 * tt + 6 * (jj + 1)],
                    xres[:, tt, 512 * jj:512 * (jj + 1)]),
                    selfw=(tt == 0), tick=(tt == n_tt - 1))
        v_attn_done = cnt["V"]
        t_attn_done = t_slab[-1]

        if debug:
            import os as _os
            _gate = int(_os.environ.get("X2GATE", str(n_slab - 1)))
            c_x2e = dma(RG, sAt, d_x2e, xres[:].rearrange("p a c -> p (a c)"),
                        waits=[(pV, v_evac[_gate]), (pT, t_attn_done) if _gate == n_slab - 1 else (pT, t_slab[_gate])],
                        max_dma_last_dim=4096)
            d_xp = nc.dram_tensor("d_xp", [128, n_tt * D], F32, kind="ExternalOutput").ap()
            _w = _os.environ.get("XPW", "ln1")
            if _w == "ln1":
                _wl = [(pV, ln1_ap[-1])]
            elif _w == "xbar":
                _wl = [(sB0, ln1_b0), (sB1, ln1_b1)]
            elif _w == "mask":
                _wl = [(pV, v_mask_done)]
            else:
                _wl = [(pV, v_apply)]
            if _w != "apply":
                dma(RG, sAt, d_xp, xres[:].rearrange("p a c -> p (a c)"),
                    waits=_wl, max_dma_last_dim=4096)
        ln2_ap = [None] * n_tt
        ln2_xb = [None] * n_tt
        ln2_aggr = [None] * n_tt

        c_at = dma(RG, sAt, at3.rearrange("p a c -> p (a c)"), at_l[:],
                   waits=[(pV, v_applies[-1])], max_dma_last_dim=8192)
        c_bp = dma(RG, sAt, bprime, bp_l[:], max_dma_last_dim=8192)
        c_bias = dma(RG, sAt, bias_sb[:, :], bias_in)

        tw = min(4, n_tt)
        nth = n_tt // tw
        nh = tw * 128
        t_rt = None
        t_ht = None

        def emit_router(tt):
            nonlocal t_rt
            sbp = sB0 if tt % 2 == 0 else sB1
            for kc in range(NKT):
                mark = (tt == n_tt - 1 and kc == NKT - 1)
                w = [(sbp, ln2_xb[tt])] if kc == 0 else ()
                if tt == 0 and kc == 0:
                    w = [(sbp, ln2_xb[tt]), (sWr, c_wrt), (pV, v_attn_done)]
                f = (lambda t, tt=tt, kc=kc: t.matmul(
                    pp[0][:, 8 * tt:8 * (tt + 1)], xnt[:, tt, kc, :],
                    wrt[:, 8 * kc:8 * (kc + 1)],
                    start=(tt == 0 and kc == 0), stop=(kc == NKT - 1),
                    skip_group_check=True))
                if mark:
                    t_rt = tmark(f, waits=w)
                else:
                    top(f, waits=w)

        def emit_ht(th):
            nonlocal t_ht
            for kc in range(NKT):
                mark = (th == nth - 1 and kc == NKT - 1)
                w = [(sAt, c_bias)] if (th == 0 and kc == 0) else ()
                f = (lambda t, th=th, kc=kc: t.matmul(
                    pp[1 + th][:, 0:nh], at3[:, kc, :],
                    xnt[:, tw * th:tw * (th + 1), kc, :], start=(kc == 0),
                    stop=(kc == NKT - 1), skip_group_check=True))
                if mark:
                    t_ht = tmark(f, waits=w)
                else:
                    top(f, waits=w)

        for tt in range(n_tt):
            wag = []
            if tt == 0:
                wag = [(pV, v_attn_done)]
                if debug:
                    wag.append((sAt, c_x2e))
            if slab_n == 256:
                ln2_aggr[tt] = vop(lambda v, tt=tt: v.bn_aggr(
                    mvall[:, 2 * tt:2 * (tt + 1)], bnall[:, 48 * tt:48 * (tt + 1)]),
                    waits=wag)
            else:
                pb = tt % 2
                for j in range(D // 512):
                    vop(lambda v, tt=tt, j=j, pb=pb: v.bn_stats(
                        bn_buf[pb][:, 6 * j:6 * (j + 1)],
                        xres[:, tt, 512 * j:512 * (j + 1)]),
                        waits=(wag if j == 0 else ()), selfw=(j == 0),
                        tick=(j == D // 512 - 1))
                ln2_aggr[tt] = vop(lambda v, tt=tt, pb=pb: v.bn_aggr(
                    mvall[:, 2 * tt:2 * (tt + 1)], bn_buf[pb][:]))
            pb = tt % 2
            a_sd = aop(lambda a, tt=tt, pb=pb: a.activation(
                sd_buf[pb][:], mvall[:, 2 * tt + 1:2 * tt + 2], AF.Sqrt, bias=epsv[:]),
                waits=[(pV, ln2_aggr[tt])])
            vop(lambda v, pb=pb: v.reciprocal(ri_buf[pb][:], sd_buf[pb][:]),
                waits=[(pA, a_sd)])
            vop(lambda v, tt=tt, pb=pb: v.tensor_scalar(
                nm_buf[pb][:], mvall[:, 2 * tt:2 * tt + 1], ri_buf[pb][:], -1.0,
                op0=ALU.mult, op1=ALU.mult))
            wap = []
            if tt >= 2:
                wap.append(((sB0 if tt % 2 == 0 else sB1), ln2_xb[tt - 2]))
            ln2_ap[tt] = vop(lambda v, tt=tt, pb=pb: v.tensor_scalar(
                xstg_ln2[pb], xres[:, tt, :], ri_buf[pb][:], nm_buf[pb][:],
                op0=ALU.mult, op1=ALU.add), waits=wap)
            w = [(pV, ln2_ap[tt]), (pT, t_attn_done)]
            ln2_xb[tt] = xbar((sB0 if tt % 2 == 0 else sB1),
                              xnt[:, tt, :, :], xstg_ln2[tt % 2], waits=w)
            emit_router(tt)
            if (tt + 1) % tw == 0:
                emit_ht((tt + 1) // tw - 1)
        ln2_b0, ln2_b1 = dcnt.get(sB0, 0), dcnt.get(sB1, 0)

        v_lg = vop(lambda v: v.tensor_copy(logits[:, 0:8 * n_tt], pp[0][:, 0:8 * n_tt]),
                   waits=[(pT, t_rt)])
        v_cmb = []
        for tt in range(n_tt):
            lg = logits[:, 8 * tt:8 * (tt + 1)]
            vop(lambda v, lg=lg: v.reduce_max(m1_t[:], lg, axis=AX.X))
            vop(lambda v, lg=lg: v.tensor_scalar(eq1_t[:], lg, m1_t[:], None,
                                                 op0=ALU.is_equal))
            vop(lambda v, lg=lg: v.scalar_tensor_tensor(l2_t[:], eq1_t[:], -1e30, lg,
                                                        op0=ALU.mult, op1=ALU.add))
            vop(lambda v: v.reduce_max(m2_t[:], l2_t[:], axis=AX.X))
            vop(lambda v: v.tensor_scalar(eq2_t[:], l2_t[:], m2_t[:], None,
                                          op0=ALU.is_equal))
            vop(lambda v: v.tensor_tensor(eqs_t[:], eq1_t[:], eq2_t[:], op=ALU.add))
            v_nm = vop(lambda v: v.tensor_scalar(nm1_t[:], m1_t[:], -1.0, None,
                                                 op0=ALU.mult))
            a_e = aop(lambda a, lg=lg: a.activation(ee_t[:], lg, AF.Exp, bias=nm1_t[:]),
                      waits=[(pV, v_nm)])
            vop(lambda v: v.tensor_tensor(cc_t[:], ee_t[:], eqs_t[:], op=ALU.mult),
                waits=[(pA, a_e)])
            vop(lambda v: v.reduce_sum(ssum_t[:], cc_t[:], axis=AX.X))
            vop(lambda v: v.reciprocal(rs_t[:], ssum_t[:]))
            v_cmb.append(vop(lambda v, tt=tt: v.tensor_scalar(
                combine[:, 8 * tt:8 * (tt + 1)], cc_t[:], rs_t[:], None, op0=ALU.mult)))
        ct_bank = [pp[3], pp[0]]
        t_ct = []
        for tt in range(n_tt):
            bank = ct_bank[tt // tw]
            w = [(pV, v_cmb[tt])]
            if tt == tw:
                w.append((pV, v_lg))
            t_ct.append(tmark(lambda t, tt=tt, bank=bank: t.matmul(
                bank[0:8, 128 * (tt % tw):128 * (tt % tw + 1)],
                combine[:, 8 * tt:8 * (tt + 1)], id_b[:],
                start=(tt % tw == 0), stop=(tt % tw == tw - 1 or tt == n_tt - 1),
                skip_group_check=True), waits=w))
        v_ct = []
        for h in range(nth):
            v_ct.append(vop(lambda v, h=h: v.tensor_copy(
                combT[0:8, nh * h:nh * (h + 1)], ct_bank[h][0:8, 0:nh]),
                waits=[(pT, t_ct[min(tw * h + tw - 1, n_tt - 1)])]))
        t_cte, v_cte = [], []
        for th in range(nth):
            w = [(pV, v_ct[th])]
            if th >= 1:
                w.append((pV, v_cte[th - 1]))
            t_cte.append(tmark(lambda t, th=th: t.matmul(
                pp[3][:, 0:nh], s16_sb[:], combT[0:8, nh * th:nh * (th + 1)],
                start=True, stop=True, skip_group_check=True),
                waits=w))
            v_cte.append(vop(lambda v, th=th: v.tensor_copy(
                hwt[:, nh * th:nh * (th + 1)], pp[3][:, 0:nh]),
                waits=[(pT, t_cte[th])]))
        v_hwt = []
        for th in range(nth):
            v_hwt.append(vop(lambda v, th=th: v.tensor_tensor(
                hwt[:, nh * th:nh * (th + 1)], pp[1 + th][:, 0:nh],
                hwt[:, nh * th:nh * (th + 1)], op=ALU.mult),
                waits=[(pT, t_ht), (pV, v_cte[th])]))
        v_prep_done = v_hwt[-1]

        t_fslab, v_fevac, c_store = [], [], []
        y2 = y_out.rearrange("(a p) k -> p a k", p=128)
        for s in range(n_slab):
            b = s % 2
            w = [(sAt, c_bias)]
            if s >= 2:
                w.append((pT, t_fslab[s - 2]))
            elif s == 0:
                w.append((pT, t_slab[n_slab - 2]))
            else:
                w.append((pT, t_attn_done))
            c_w = dma(RG, (sW0 if b == 0 else sW1),
                      wts[:, b, :, :].rearrange("p a c -> p (a c)"),
                      wtb[s], waits=w, max_dma_last_dim=8192)
            c_wcast.append(c_w)
            bps = (s + 1) % 2
            if s < 2:
                for tt in range(n_tt):
                    w2 = [((sB0 if tt % 2 == 0 else sB1), ln2_xb[tt])]
                    if tt == 0:
                        w2.append(((sW0 if b == 0 else sW1), c_w))
                        if s == 0:
                            w2.append((pV, v_attn_done))
                        else:
                            w2.append((pV, v_prep_done))
                    for kt in range(NKT):
                        bank = pp[4 * bps + tt // 2]
                        top(lambda t, kt=kt, tt=tt, b=b, bank=bank: t.matmul(
                            bank[:, 256 * (tt % 2):256 * (tt % 2) + slab_n],
                            xnt[:, tt, kt, :], wts[:, b, kt, :],
                            start=(kt == 0 and tt % 2 == 0), stop=False,
                            skip_group_check=True), waits=(w2 if kt == 0 else ()))
            else:
                w = [((sW0 if b == 0 else sW1), c_w), (pV, v_fevac[s - 2])]
                first = True
                for kt in range(NKT):
                    for tt in range(n_tt):
                        bank = pp[4 * bps + tt // 2]
                        top(lambda t, kt=kt, tt=tt, b=b, bank=bank: t.matmul(
                            bank[:, 256 * (tt % 2):256 * (tt % 2) + slab_n],
                            xnt[:, tt, kt, :], wts[:, b, kt, :],
                            start=(kt == 0 and tt % 2 == 0), stop=False,
                            skip_group_check=True), waits=(w if first else ()))
                        first = False
            for tt in range(n_tt):
                bank = pp[4 * bps + tt // 2]
                wl = [(pV, v_prep_done)] if (s == 0 and tt == 0) else ()
                top(lambda t, tt=tt, s=s, bank=bank: t.matmul(
                    bank[:, 256 * (tt % 2):256 * (tt % 2) + slab_n],
                    hwt[:, 128 * tt:128 * (tt + 1)],
                    bprime[:, slab_n * s:slab_n * (s + 1)],
                    start=False, stop=False, skip_group_check=True), waits=wl)
                mark = (tt == n_tt - 1)
                f = (lambda t, tt=tt, s=s, bank=bank: t.matmul(
                    bank[:, 256 * (tt % 2):256 * (tt % 2) + slab_n],
                    ones_sb[:], bias_sb[:, slab_n * s:slab_n * (s + 1)],
                    start=False, stop=True, skip_group_check=True))
                if mark:
                    t_fslab.append(tmark(f))
                else:
                    top(f)
            last = None
            for tt in range(n_tt):
                bank = pp[4 * bps + tt // 2]
                w2 = []
                if tt == 0:
                    w2 = [(pT, t_fslab[s])]
                    if s >= 1:
                        w2.append((sSt, c_store[s - 1]))
                last = vop(lambda v, tt=tt, s=s, bank=bank: v.tensor_tensor(
                    ost[:, tt, :], bank[:, 256 * (tt % 2):256 * (tt % 2) + slab_n],
                    xres[:, tt, slab_n * s:slab_n * (s + 1)], op=ALU.add),
                    waits=w2, selfw=(tt == 0), tick=(tt == n_tt - 1))
            v_fevac.append(last)
            c_store.append(dma(RY, sSt,
                               y2[:, :, slab_n * s:slab_n * (s + 1)], ost[:, 0:n_tt, :],
                               waits=[(pV, v_fevac[s])]))
        c_final = dcnt[sSt]
        if debug:
            wfin = [(sSt, c_final), (pV, cnt["V"]), (pA, cnt["A"]), (pT, cnt["T"])]
            dma(RG, sAt, d_xres, xres[:].rearrange("p a c -> p (a c)"),
                waits=wfin, max_dma_last_dim=4096)
            dma(RG, sAt, d_comb, combine[:], waits=wfin)
            dma(RG, sAt, d_hwt, hwt[:], waits=wfin)
            c_final2 = dcnt[sAt]

        with nc.Block() as blk:
            @blk.gpsimd
            def _(g):
                for f in RG:
                    f(g)

            @blk.sync
            def _(s):
                for f in RY:
                    f(s)
                s.wait_ge(sSt, c_final)
                if debug:
                    s.wait_ge(sAt, c_final2)

            @blk.vector
            def _(v):
                for f in RV:
                    f(v)

            @blk.scalar
            def _(a):
                for f in RA:
                    f(a)

            @blk.tensor
            def _(t):
                for f in RT:
                    f(t)

    return nc


def _lay_weights(attn_w, base_w, a_w, b_w, wr_w, n_slab=16, slab_n=256):
    wt = np.ascontiguousarray(
        attn_w.T.reshape(NKT, 128, n_slab, slab_n).transpose(2, 1, 0, 3)
    ).reshape(n_slab, 128, NKT * slab_n)
    wtb = np.ascontiguousarray(
        base_w.T.reshape(NKT, 128, n_slab, slab_n).transpose(2, 1, 0, 3)
    ).reshape(n_slab, 128, NKT * slab_n)
    at = np.ascontiguousarray(
        a_w.reshape(128, NKT, 128).transpose(2, 1, 0)).reshape(128, D)
    bp = np.ascontiguousarray(b_w.transpose(0, 2, 1)).reshape(128, D)
    wrt = np.ascontiguousarray(
        wr_w.reshape(8, NKT, 128).transpose(2, 1, 0)).reshape(128, NKT * 8)
    return wt, wtb, at, bp, wrt


def _consts():
    p = np.arange(128)
    identb = np.eye(128, dtype=np.float32).astype(ml_dtypes.bfloat16)
    identf = None
    ones1 = np.ones((1, 128), dtype=ml_dtypes.bfloat16)
    bdiag4 = (p[:, None] // 32 == np.arange(4)[None, :]).astype(np.float32).astype(ml_dtypes.bfloat16)
    s4m = (np.arange(4)[:, None] == p[None, :] // 32).astype(np.float32).astype(ml_dtypes.bfloat16)
    s16 = (2.0 * (np.arange(8)[:, None] == p[None, :] // 16)).astype(np.float32).astype(ml_dtypes.bfloat16)
    epsv = np.full((128, 1), 1e-5, dtype=np.float32)
    return identb, identf, ones1, bdiag4, s4m, s16, epsv


def kernel(L0_attn_data, L0_attn_weight, L0_attn_mask, L0_ffn_W_base,
           L0_ffn_b_base, L0_ffn_W_router, L0_ffn_A, L0_ffn_B):
    x2d = np.ascontiguousarray(np.asarray(L0_attn_data, dtype=np.float32).reshape(8192, D))
    wt, wtb, at, bp, wrt = _lay_weights(
        np.asarray(L0_attn_weight, dtype=np.float32),
        np.asarray(L0_ffn_W_base, dtype=np.float32),
        np.asarray(L0_ffn_A, dtype=np.float32).reshape(128, D),
        np.asarray(L0_ffn_B, dtype=np.float32),
        np.asarray(L0_ffn_W_router, dtype=np.float32))
    mask = np.asarray(L0_attn_mask, dtype=np.int32)
    bias = np.asarray(L0_ffn_b_base, dtype=np.float32).reshape(1, D)
    identb, identf, ones1, bdiag4, s4m, s16, epsv = _consts()

    nc = build_core_program()
    shared = dict(wta=wt, wtb=wtb, at_l=at, bp_l=bp, wrt_l=wrt,
                  bias_in=bias, epsv_in=epsv, identb=identb,
                  ones1=ones1, bdiag4=bdiag4, s4m=s4m, s16x2=s16)
    in_maps = []
    for c in range(N_CORES):
        m = dict(shared)
        m["x_in"] = np.ascontiguousarray(x2d[1024 * c:1024 * (c + 1), :])
        m["mask_in"] = np.ascontiguousarray(mask[512 * c:512 * (c + 1), :])
        in_maps.append(m)
    res = run_bass_kernel_spmd(nc, in_maps, core_ids=list(range(N_CORES)))
    out = np.concatenate([r["y_out"] for r in res.results], axis=0)
    return out.reshape(2, 4096, 4096)
